# version 09
# baseline (speedup 1.0000x reference)
import contextlib
import numpy as np

import concourse.bass as bass
import concourse.mybir as mybir
from concourse.bass_utils import run_bass_kernel_spmd

F32 = mybir.dt.float32
BF16 = mybir.dt.bfloat16
I8 = mybir.dt.int8
AF = mybir.ActivationFunctionType
ALU = mybir.AluOpType
AX = mybir.AxisListType

NCORES = 8
T = 2048
C = 1024
NH = 16
HD = 64
IFF = 2816
NT = T // 128
NCC = C // 128
NIC = IFF // 128
TG_ALL = 8
MASK_NEG = -1.0e5
EPS = 1e-5
QMAX = 127.0


class Op:
    __slots__ = ("engine", "waits", "emit", "incs")

    def __init__(self, engine, waits, emit, incs):
        self.engine = engine
        self.waits = waits
        self.emit = emit
        self.incs = incs


def build_program(debug=False, INTERLEAVE_AF=True, INTERLEAVE_WQ=True):
    nc = bass.Bass("TRN2", target_bir_lowering=False)

    x_d = nc.dram_tensor("x", [2, T, C], F32, kind="ExternalInput").ap()
    wt_d = nc.dram_tensor("wt", [C, 3 * C], F32, kind="ExternalInput").ap()
    bias_d = nc.dram_tensor("bias_sb", [128, 24], F32, kind="ExternalInput").ap()
    biasr_d = nc.dram_tensor("bias_row", [1, 3 * C], F32, kind="ExternalInput").ap()
    wg_n_d = nc.dram_tensor("wg_n", [IFF, C], F32, kind="ExternalInput").ap()
    wu_n_d = nc.dram_tensor("wu_n", [IFF, C], F32, kind="ExternalInput").ap()
    wd_n_d = nc.dram_tensor("wd_n", [C, IFF], F32, kind="ExternalInput").ap()
    wg_t_d = nc.dram_tensor("wg_t", [C, IFF], F32, kind="ExternalInput").ap()
    wu_t_d = nc.dram_tensor("wu_t", [C, IFF], F32, kind="ExternalInput").ap()
    wd_t_d = nc.dram_tensor("wd_t", [IFF, C], F32, kind="ExternalInput").ap()
    clip_d = nc.dram_tensor("clip2", [1, 2], F32, kind="ExternalInput").ap()
    ident_d = nc.dram_tensor("ident", [128, 128], F32, kind="ExternalInput").ap()
    identb_d = nc.dram_tensor("identb", [128, 128], BF16, kind="ExternalInput").ap()
    mask_d = nc.dram_tensor("maskbig", [128, 896], F32, kind="ExternalInput").ap()
    ones_d = nc.dram_tensor("ones1", [1, 128], F32, kind="ExternalInput").ap()
    out_d = nc.dram_tensor("out", [2, T, C], F32, kind="ExternalOutput").ap()

    qkvT_d = nc.dram_tensor("qkvT_s", [2, 2 * C, T], F32).ap()
    v_d = nc.dram_tensor("v_s", [2, T, C], F32).ap()
    y_d = nc.dram_tensor("y_s", [2, NH, HD + 1, T], F32).ap()
    out1_d = nc.dram_tensor("out1_s", [2, T, C], F32).ap()
    ln2_d = nc.dram_tensor("ln2_s", [2, T, C], F32).ap()
    h_d = nc.dram_tensor("h_s", [IFF, 2 * T], F32).ap()
    srow_d = nc.dram_tensor("srow_s", [IFF], F32).ap()
    cci_d = nc.dram_tensor("cc_in", [1, 1], F32).ap()
    cco_d = nc.dram_tensor("cc_out", [1, 1], F32).ap()

    SB_BASE = 16544

    def sb(name, shape, dtype, off):
        return nc.alloc_sbuf_tensor_at(name, list(shape), dtype,
                                       offset=SB_BASE + off).ap()

    ident = sb("c_ident", [128, 128], F32, 0)
    maskbig = sb("c_mask", [128, 896], F32, 512)
    bias_sb = sb("c_bias", [128, 24], F32, 4096)
    ones1 = sb("c_ones", [1, 128], F32, 4224)
    identb = sb("c_identb", [128, 128], BF16, 4736)
    stats = sb("c_stats", [128, 128], F32, 4992)
    sm = sb("c_small", [128, 32], F32, 5504)
    sc = sb("c_scal", [1, 16], F32, 5632)
    scl = sb("c_scl", [128, 112], F32, 5696)

    wtp = sb("a_wtp", [128, 8 * 1024], F32, 6144)
    ln1T = sb("a_ln1T", [128, 8 * 2048], F32, 38912)
    xt2 = [sb(f"a_xt{i}", [128, 1024], F32, 104448 + 4096 * i) for i in range(2)]
    lnt2 = [sb(f"a_lnt{i}", [128, 1024], F32, 112640 + 4096 * i) for i in range(2)]
    sqjunk_a = sb("a_sqj", [128, 1024], F32, 120832)
    stg2 = [sb(f"a_stg{i}", [128, 2048], F32, 124928 + 8192 * i) for i in range(2)]

    qb2 = [sb(f"b_q{i}", [64, 2048], F32, 141312 + 8192 * i) for i in range(2)]
    kb2 = [sb(f"b_k{i}", [64, 2048], F32, 157696 + 8192 * i) for i in range(2)]
    vb2 = [sb(f"b_v{i}", [64, 2048], F32, 174080 + 8192 * i) for i in range(2)]
    vtok2 = [sb(f"b_vt{i}", [128, 16 * 65], F32, 190464 + 4160 * i) for i in range(2)]
    pbD = [sb(f"b_p{i}", [128, 1024], F32, 198784 + 4096 * i) for i in range(2)]
    yst2 = [sb(f"b_yst{i}", [128, 512], F32, 206976 + 2048 * i) for i in range(2)]

    fyTl2 = [sb(f"f_yTl{i}", [65, 16 * 128], F32, 141312 + 8192 * i)
             for i in range(2)]
    fo1b2 = [sb(f"f_o1{i}", [128, 1024], F32, 157696 + 4096 * i)
             for i in range(2)]
    fsq = sb("f_sq", [128, 1024], F32, 174080)
    fxt2 = [sb(f"f_xt{i}", [128, 1024], F32, 178176 + 4096 * i)
            for i in range(2)]
    flnt2 = [sb(f"f_lnt{i}", [128, 1024], F32, 186368 + 4096 * i)
             for i in range(2)]

    wcg = sb("p_wcg", [128, 8 * IFF], BF16, 6144)
    wcu = sb("p_wcu", [128, 8 * IFF], BF16, 51200)
    xcT = sb("p_xcT", [128, 8 * 4096], BF16, 96256)
    hb3 = [sb(f"p_hb{i}", [128, 512], F32, 161792 + 2048 * i) for i in range(3)]
    sgb2 = [sb(f"p_sg{i}", [128, 512], F32, 167936 + 2048 * i) for i in range(2)]
    wnt2 = [sb(f"p_wnt{i}", [128, IFF], F32, 96256 + 11264 * i) for i in range(2)]
    wq_cf = sb("p_wqcf", [128, 1024], F32, 118784)
    ci8 = sb("p_ci8", [128, 1024], I8, 122880)
    xc8 = sb("p_xc8", [128, 1024], I8, 180224)
    xcf = sb("p_xcf", [128, 1024], F32, 181248)
    ln2l2 = [sb(f"p_ln2l{i}", [128, 1024], F32, 172032 + 4096 * i) for i in range(2)]
    wcd = sb("p_wcd", [128, NIC * 1024], BF16, 96256)
    nhb2 = [sb(f"p_nhb{i}", [128, NIC * 512], BF16,
               6144 if i == 0 else 69632) for i in range(2)]
    dsc2 = [sb(f"p_dsc{i}", [128, 512], F32, 28672 + 2048 * i) for i in range(2)]
    o1c4 = [sb(f"p_o1c{i}", [128, 1024], F32, 32768 + 4096 * i) for i in range(4)]
    o2b4 = [sb(f"p_o2{i}", [128, 1024], F32, 49152 + 4096 * i) for i in range(4)]
    hl2 = [sb(f"p_hl{i}", [128, 512], F32, 65536 + 2048 * i) for i in range(2)]
    wnt3 = [sb(f"p_wnt3{i}", [128, IFF], F32, 180224 + 11264 * i) for i in range(2)]
    ci83 = sb("p_ci83", [128, IFF], I8, 206848)
    wq_cf3 = sb("p_wqcf3", [128, IFF], F32, 141312)

    psd0 = nc.alloc_psum_tensor("psd0", [128, 1024], F32).ap()
    psd1 = nc.alloc_psum_tensor("psd1", [128, 1024], F32).ap()
    psd2 = nc.alloc_psum_tensor("psd2", [128, 1024], F32).ap()
    _pss = [nc.alloc_psum_tensor(f"ps{i}", [128, 512], F32).ap()
            for i in range(6, 8)]
    ps = [psd0[:, 0:512], psd0[:, 512:1024], psd1[:, 0:512],
          psd1[:, 512:1024], psd2[:, 0:512], psd2[:, 512:1024]] + _pss
    psd = [psd0, psd1, psd2]
    ps7 = ps[7]

    ops = []

    with contextlib.ExitStack() as stack:
        class Ctr:
            def __init__(self, name):
                self.sem = stack.enter_context(nc.semaphore(name))
                self.n = 0
                self.name = name

        TE = Ctr("te")
        VE = Ctr("ve")
        SE = Ctr("se")
        CC = Ctr("cc")
        _dma_ctrs = {}
        _store_ctrs = []

        def dma_ctr(name, store=False):
            if name not in _dma_ctrs:
                c = Ctr(name)
                _dma_ctrs[name] = c
                if store:
                    _store_ctrs.append(c)
            return _dma_ctrs[name]

        guards = {}

        def guard(key):
            return list(guards.get(key, ()))

        def set_guard(key, *pairs):
            guards[key] = [(c, v) for (c, v) in pairs]

        def all_stores():
            return [(c, c.n) for c in _store_ctrs]

        def fence():
            return [(TE, TE.n), (VE, VE.n), (SE, SE.n)] + all_stores()

        def E(engine, emit, waits=(), inc=None, k=1):
            incs = []
            if inc is not None:
                inc.n += k
                incs = [(inc.sem, k)]
            ws = [(c.sem, v, c.name) for (c, v) in waits if v > 0]
            ops.append(Op(engine, ws, emit, incs))

        LDc = dma_ctr("ldc")
        E("sync", lambda s: s.dma_start(out=ident[:, :], in_=ident_d[:, :]), inc=LDc, k=16)
        E("sync", lambda s: s.dma_start(out=identb[:, :], in_=identb_d[:, :]), inc=LDc, k=16)
        E("sync", lambda s: s.dma_start(out=maskbig[:, :], in_=mask_d[:, :]), inc=LDc, k=16)
        E("sync", lambda s: s.dma_start(out=bias_sb[:, :], in_=bias_d[:, :]), inc=LDc, k=16)
        E("sync", lambda s: s.dma_start(out=ones1[:, :], in_=ones_d[:, :]), inc=LDc, k=16)
        E("sync", lambda s: s.dma_start(out=sc[0:1, 0:2], in_=clip_d[:, :]), inc=LDc, k=16)
        consts_ld = LDc.n
        E("scalar", lambda s: s.activation(sc[0:1, 2:4], sc[0:1, 0:2], AF.Sigmoid),
          waits=[(LDc, consts_ld)], inc=SE)
        se_sig = SE.n
        E("vector", lambda v: v.memset(sm[:, :], 0.0),
          waits=[(LDc, consts_ld)], inc=VE)
        E("vector", lambda v: v.memset(stats[:, :], 0.0), inc=VE)

        def vdrain():
            E("vector", lambda v: v.drain(), inc=VE)

        def ln_tile(xtile, lntile, tcol, extra_waits, so=0, tb0=6, sqb=None):
            tb = tb0 + 3 * (tcol % 2)
            sqjunk = sqb if sqb is not None else sqjunk_a
            tcol = tcol + so
            E("vector", lambda v, xt=xtile, c=tcol: v.tensor_reduce(
                stats[:, c:c + 1], xt[:, :], AX.X, ALU.add),
              waits=extra_waits, inc=VE)
            vdrain()
            E("vector", lambda v, c=tcol: v.tensor_scalar(
                stats[:, 32 + c:33 + c], stats[:, c:c + 1], -1.0 / C, None,
                ALU.mult), inc=VE)
            vdrain()
            E("vector", lambda v, xt=xtile, lt=lntile, c=tcol: v.tensor_scalar(
                lt[:, :], xt[:, :], stats[:, 32 + c:33 + c], None, ALU.add),
              inc=VE)
            ve_d = VE.n
            E("scalar", lambda s, lt=lntile, c=tcol: s.activation(
                sqjunk[:, :], lt[:, :], AF.Square,
                accum_out=stats[:, 16 + c:17 + c]),
              waits=[(VE, ve_d)], inc=SE)
            se_sq = SE.n
            E("vector", lambda v, c=tcol, tb=tb: v.tensor_scalar(
                sm[:, tb + 1:tb + 2], stats[:, 16 + c:17 + c], 1.0 / C, EPS,
                ALU.mult, ALU.add), waits=[(SE, se_sq)], inc=VE)
            ve_var = VE.n
            E("scalar", lambda s, tb=tb: s.activation(
                sm[:, tb + 2:tb + 3], sm[:, tb + 1:tb + 2], AF.Sqrt),
              waits=[(VE, ve_var)], inc=SE)
            se_sqrt = SE.n
            E("vector", lambda v, c=tcol, tb=tb: v.reciprocal(
                stats[:, 48 + c:49 + c], sm[:, tb + 2:tb + 3]),
              waits=[(SE, se_sqrt)], inc=VE)
            vdrain()
            E("vector", lambda v, c=tcol, tb=tb: v.tensor_scalar(
                sm[:, tb:tb + 1], stats[:, 48 + c:49 + c],
                stats[:, 48 + c:49 + c], None, ALU.mult), inc=VE)
            vdrain()
            E("vector", lambda v, tb=tb: v.tensor_tensor(
                sm[:, tb:tb + 1], sm[:, tb:tb + 1], sm[:, tb + 1:tb + 2],
                ALU.mult), inc=VE)
            vdrain()
            E("vector", lambda v, tb=tb: v.tensor_scalar(
                sm[:, tb:tb + 1], sm[:, tb:tb + 1], -0.5, 1.5,
                ALU.mult, ALU.add), inc=VE)
            vdrain()
            E("vector", lambda v, c=tcol, tb=tb: v.tensor_tensor(
                stats[:, 48 + c:49 + c], stats[:, 48 + c:49 + c],
                sm[:, tb:tb + 1], ALU.mult), inc=VE)
            vdrain()
            E("vector", lambda v, lt=lntile, c=tcol: v.tensor_scalar(
                lt[:, :], lt[:, :], stats[:, 48 + c:49 + c], None, ALU.mult),
              inc=VE)
            return VE.n

        class MaxAccum:

            def __init__(self, dst, stage0=14):
                self.dst = dst
                self.s0 = stage0
                self.i = 0
                self.pending = None

            def reduce(self, src_ap, waits=()):
                col = self.s0 + self.i % 2
                prev = self.pending
                self.pending = None
                E("vector", lambda v, src=src_ap, col=col: v.tensor_reduce(
                    sm[:, col:col + 1], src, AX.X, ALU.max,
                    apply_absolute_value=True), waits=list(waits), inc=VE)
                if prev is not None:
                    col2, d = prev, self.dst
                    E("vector", lambda v, col=col2, d=d: v.tensor_tensor(
                        sm[:, d:d + 1], sm[:, d:d + 1], sm[:, col:col + 1],
                        ALU.max), inc=VE)
                self.pending = col
                self.i += 1

            def flush_one(self):
                if self.pending is not None:
                    col, d = self.pending, self.dst
                    E("vector", lambda v, col=col, d=d: v.tensor_tensor(
                        sm[:, d:d + 1], sm[:, d:d + 1], sm[:, col:col + 1],
                        ALU.max), inc=VE)
                    self.pending = None

            def finish(self):
                vdrain()
                self.flush_one()
                return VE.n

        def bcast_col(src_ap, dst_col, extra_waits):
            E("tensor", lambda te, src=src_ap: te.matmul(
                ps7[:, 0:1], ones1[:, :], src, start=True, stop=True),
              waits=extra_waits + guard("ps7"), inc=TE)
            E("vector", lambda v, c=dst_col: v.tensor_copy(
                sm[:, c:c + 1], ps7[:, 0:1]), waits=[(TE, TE.n)], inc=VE)
            set_guard("ps7", (VE, VE.n))
            return VE.n

        def part_max(col, out_sc_col):
            vdrain()
            E("tensor", lambda te, c=col: te.transpose(
                ps7[0:1, 0:128], sm[:, c:c + 1], ident[:, :]),
              waits=[(VE, VE.n)] + guard("ps7"), inc=TE)
            E("vector", lambda v, oc=out_sc_col: v.tensor_reduce(
                sc[0:1, oc:oc + 1], ps7[0:1, 0:128], AX.X, ALU.max),
              waits=[(TE, TE.n)], inc=VE)
            set_guard("ps7", (VE, VE.n))
            return VE.n

        def phase_A_gen(b, first_fence):
            E("vector", lambda v: v.memset(sm[:, 5:6], 0.0), inc=VE)
            qkmax = MaxAccum(5)
            ve_ln1T_tg = [0] * 4
            for t in range(NT):
                xt, lt = xt2[t % 2], lnt2[t % 2]
                LDx = dma_ctr(f"ldx{t % 2}")
                lw = guard(("xt", t % 2))
                if t == 0 and first_fence:
                    lw = lw + first_fence
                E("sync", lambda s, b=b, t=t, xt=xt: s.dma_start(
                    out=xt[:, :], in_=x_d[b, t * 128:(t + 1) * 128, :]),
                  waits=lw, inc=LDx, k=16)
                ld_x = (LDx, LDx.n)
                lnw = [ld_x] + guard(("lnt", t % 2))
                ve_ln = ln_tile(xt, lt, t, lnw)
                set_guard(("xt", t % 2), (VE, ve_ln), (SE, SE.n))
                for half in range(2):
                    bank = ps[4 + half]
                    for q in range(4):
                        cchunk = half * 4 + q
                        tw = ()
                        if q == 0:
                            tw = [(VE, ve_ln)] + guard(("psT", 4 + half))
                            if t == 0 and first_fence:
                                tw = tw + first_fence
                        E("tensor", lambda te, lt=lt, bank=bank, q=q, c=cchunk:
                          te.transpose(bank[:, q * 128:(q + 1) * 128],
                                       lt[:, c * 128:(c + 1) * 128], ident[:, :]),
                          waits=tw,
                          inc=(TE if q == 3 else None))
                    te_tr = TE.n
                    E("vector", lambda v, bank=bank, t=t, half=half:
                      v.tensor_copy(
                          ln1T.rearrange("p (c w) -> p c w", c=8)[
                              :, half * 4:(half + 1) * 4, t * 128:(t + 1) * 128],
                          bank.rearrange("p (q w) -> p q w", q=4)),
                      waits=[(TE, te_tr)], inc=VE)
                    set_guard(("psT", 4 + half), (VE, VE.n))
                set_guard(("lnt", t % 2), (TE, TE.n))
                if t % 4 == 3:
                    ve_ln1T_tg[t // 4] = VE.n
                yield
            for piece in range(2):
                for hp in range(2):
                    LDw = dma_ctr(f"ldw{hp}")
                    E("sync", lambda s, piece=piece, hp=hp: s.dma_start(
                        out=wtp.rearrange("p (c o) -> p c o", c=8)[
                            :, :, hp * 512:(hp + 1) * 512],
                        in_=wt_d.rearrange("(c p) o -> p c o", p=128)[
                            :, :, piece * 1024 + hp * 512:
                            piece * 1024 + hp * 512 + 512]),
                      waits=guard(("wtp", hp)), inc=LDw, k=16)
                ld_ws = [(dma_ctr("ldw0"), dma_ctr("ldw0").n),
                         (dma_ctr("ldw1"), dma_ctr("ldw1").n)]
                for oc in range(8):
                    o_abs = piece * 1024 + oc * 128
                    j = piece * 8 + oc
                    stg = stg2[j % 2]
                    for tg in range(4):
                        bank = ps[tg]
                        mm_w = ([ld_ws[oc // 4], (VE, ve_ln1T_tg[tg])]
                                + guard(("psA", tg)))
                        for cc in range(8):
                            E("tensor", lambda te, bank=bank, oc=oc, cc=cc, tg=tg:
                              te.matmul(bank[:, :],
                                        wtp[:, cc * 1024 + oc * 128:
                                            cc * 1024 + oc * 128 + 128],
                                        ln1T[:, cc * 2048 + tg * 512:
                                             cc * 2048 + tg * 512 + 512],
                                        start=(cc == 0), stop=(cc == 7)),
                              waits=mm_w if cc == 0 else (),
                              inc=(TE if cc == 7 else None))
                        te_mm = TE.n
                        bcol = piece * 8 + oc
                        ew = [(TE, te_mm)]
                        if tg == 0:
                            ew += guard(("stg", j % 2))
                        E("scalar", lambda s, bank=bank, stg=stg, tg=tg, bcol=bcol:
                          s.activation(stg[:, tg * 512:(tg + 1) * 512], bank[:, :],
                                       AF.Identity, bias=bias_sb[:, bcol:bcol + 1]),
                          waits=ew, inc=SE)
                        set_guard(("psA", tg), (SE, SE.n))
                        se_evac = SE.n
                        qkmax.reduce(stg[:, tg * 512:(tg + 1) * 512],
                                     waits=[(SE, se_evac)])
                    se_all = SE.n
                    ve_all = VE.n
                    STg = dma_ctr(f"stg{j % 2}", store=True)
                    E("gpsimd", lambda g, b=b, o_abs=o_abs, stg=stg: g.dma_start(
                        out=qkvT_d[b, o_abs:o_abs + 128, :], in_=stg[:, :]),
                      waits=[(SE, se_all)], inc=STg, k=16)
                    set_guard(("stg", j % 2), (STg, STg.n), (VE, ve_all))
                    if oc == 3 or oc == 7:
                        set_guard(("wtp", oc // 4), (TE, TE.n))
                    yield
            LDbr = dma_ctr("ldbr")
            ve_lnlast = VE.n
            se_lnlast = SE.n
            E("sync", lambda s: s.dma_start(
                out=sqjunk_a[0:1, 0:1024], in_=biasr_d[0:1, 2048:3072]),
              waits=[(SE, se_lnlast)], inc=LDbr, k=16)
            ld_br = (LDbr, LDbr.n)
            for hp in range(2):
                LDw = dma_ctr(f"ldw{hp}")
                E("sync", lambda s, hp=hp: s.dma_start(
                    out=wtp.rearrange("p (c o) -> p c o", c=8)[
                        :, :, hp * 512:(hp + 1) * 512],
                    in_=wt_d.rearrange("(c p) o -> p c o", p=128)[
                        :, :, 2048 + hp * 512:2048 + hp * 512 + 512]),
                  waits=guard(("wtp", hp)), inc=LDw, k=16)
            ld_ws = [(dma_ctr("ldw0"), dma_ctr("ldw0").n),
                     (dma_ctr("ldw1"), dma_ctr("ldw1").n)]
            for t in range(NT):
                j = t
                stg = stg2[j % 2]
                for og in range(2):
                    bank = ps[(2 * t + og) % 4]
                    mm_w = ([ld_ws[og], ld_br, (VE, ve_ln1T_tg[t // 4])]
                            + guard(("psA", (2 * t + og) % 4)))
                    for cc in range(8):
                        E("tensor", lambda te, bank=bank, t=t, og=og, cc=cc:
                          te.matmul(bank[:, :],
                                    ln1T[:, cc * 2048 + t * 128:
                                         cc * 2048 + t * 128 + 128],
                                    wtp[:, cc * 1024 + og * 512:
                                        cc * 1024 + og * 512 + 512],
                                    start=(cc == 0), stop=False),
                          waits=mm_w if cc == 0 else (), inc=None)
                    E("tensor", lambda te, bank=bank, og=og:
                      te.matmul(bank[:, :], ones1[0:1, 0:128],
                                sqjunk_a[0:1, og * 512:(og + 1) * 512],
                                start=False, stop=True), inc=TE)
                    te_mm = TE.n
                    ew = [(TE, te_mm)]
                    if og == 0:
                        ew += guard(("stg", j % 2))
                    E("scalar", lambda s, bank=bank, stg=stg, og=og:
                      s.activation(stg[:, og * 512:(og + 1) * 512], bank[:, :],
                                   AF.Identity),
                      waits=ew, inc=SE)
                    set_guard(("psA", (2 * t + og) % 4), (SE, SE.n))
                se_all = SE.n
                STg = dma_ctr(f"stg{j % 2}", store=True)
                E("gpsimd", lambda g, b=b, t=t, stg=stg: g.dma_start(
                    out=v_d[b, t * 128:(t + 1) * 128, :], in_=stg[:, 0:1024]),
                  waits=[(SE, se_all)], inc=STg, k=16)
                set_guard(("stg", j % 2), (STg, STg.n))
                if t == 3 or t == 11:
                    yield
            set_guard(("wtp", 0), (TE, TE.n))
            set_guard(("wtp", 1), (TE, TE.n))
            qkmax.finish()
            part_max(5, 4)
            vdrain()
            E("vector", lambda v: v.tensor_tensor(
                sc[0:1, 5:6], sc[0:1, 4:5], sc[0:1, 4:5], ALU.mult), inc=VE)
            vdrain()
            E("vector", lambda v: v.tensor_scalar(
                sc[0:1, 5:6], sc[0:1, 5:6], -8.0, -80.0, ALU.mult, ALU.max),
              inc=VE)
            vdrain()
            ve_b = VE.n
            bcast_col(sc[0:1, 5:6], 0, [(VE, ve_b)])

        def phase_B_gen(b, ve_bias, st_after_A):
            for bi in range(2):
                E("vector", lambda v, vtok=vtok2[bi]: v.memset(
                    vtok.rearrange("p (c d) -> p c d", d=65)[:, :, 64:65], 1.0),
                  waits=guard(("vtok", bi)), inc=VE)
                set_guard(("vtok_ones", bi), (VE, VE.n))
            for h in range(NH):
                bi = h % 2
                qb, kb, vtok = qb2[bi], kb2[bi], vtok2[bi]
                LDq = dma_ctr(f"ldq{bi}")
                LDv = dma_ctr(f"ldv{bi}")
                lw = guard(("bqkv", bi))
                if h == 0:
                    lw = lw + st_after_A
                E("sync", lambda s, b=b, h=h, qb=qb: s.dma_start(
                    out=qb[:, :], in_=qkvT_d[b, h * 64:h * 64 + 64, :]),
                  waits=lw, inc=LDq, k=16)
                E("sync", lambda s, b=b, h=h, kb=kb: s.dma_start(
                    out=kb[:, :], in_=qkvT_d[b, 1024 + h * 64:1024 + h * 64 + 64, :]),
                  inc=LDq, k=16)
                ld_qkv = (LDq, LDq.n)
                ve_ones = guard(("vtok_ones", bi))[0][1]
                E("sync", lambda s, b=b, h=h, vtok=vtok: s.dma_start(
                    out=vtok.rearrange("p (c d) -> p c d", d=65)[:, :, 0:64],
                    in_=v_d.rearrange("b (c p) o -> b p c o", p=128)[
                        b, :, :, h * 64:h * 64 + 64]),
                  waits=guard(("vtok", bi)) + [(VE, ve_ones)], inc=LDv, k=16)
                ld_v = (LDv, LDv.n)
                ybanks = [ps[6], ps[7]]
                if h == 0:
                    for i in (0, 1, 2):
                        set_guard(("psSD", i),
                                  *(guard(("psA", 2 * i)) +
                                    guard(("psA", 2 * i + 1))
                                    if i < 2 else
                                    guard(("psT", 4)) + guard(("psT", 5))))
                    set_guard(("psY", 1), *(guard(("psY", 1)) + guard("ps7")))
                for g in range(4):
                    nk = 4 * (g + 1)
                    npair = nk // 2
                    se_exp = [0] * npair
                    te_av = [0] * nk
                    ybank = ybanks[g % 2]

                    def emit_S(kc):
                        pr = kc // 2
                        bank = psd[pr % 3][:, (kc % 2) * 512:(kc % 2) * 512 + 512]
                        w = [ld_qkv]
                        if kc % 2 == 0:
                            w += guard(("psSD", pr % 3))
                        E("tensor", lambda te, kb=kb, qb=qb, bank=bank, kc=kc, g=g:
                          te.matmul(bank, kb[:, kc * 128:(kc + 1) * 128],
                                    qb[:, g * 512:(g + 1) * 512],
                                    start=True, stop=True), waits=w, inc=TE)
                        te_s = TE.n
                        if kc >= 4 * g:
                            r = kc * 128 - g * 512
                            E("vector", lambda v, bank=bank, r=r: v.tensor_tensor(
                                bank, bank,
                                maskbig[:, 384 - r:384 - r + 512], ALU.add),
                              waits=[(TE, te_s)], inc=VE)
                            return (VE, VE.n)
                        return (TE, te_s)

                    def emit_exp(pr, ready):
                        w = list(ready) + [(VE, ve_bias)]
                        if pr >= 2:
                            w.append((TE, te_av[2 * (pr - 2) + 1]))
                        E("scalar", lambda s, pr=pr: s.activation(
                            pbD[pr % 2][:, :], psd[pr % 3][:, :], AF.Exp,
                            scale=0.125, bias=sm[:, 0:1]), waits=w, inc=SE)
                        se_exp[pr] = SE.n
                        set_guard(("psSD", pr % 3), (SE, SE.n))

                    def emit_AV(kc):
                        pr = kc // 2
                        w = [(SE, se_exp[pr]), ld_v]
                        if kc == 0:
                            w += guard(("psY", g % 2))
                        E("tensor", lambda te, vtok=vtok, kc=kc, nk=nk, yb=ybank,
                          pr=pr:
                          te.matmul(yb[0:65, :], vtok[:, kc * 65:kc * 65 + 65],
                                    pbD[pr % 2][:, (kc % 2) * 512:
                                                 (kc % 2) * 512 + 512],
                                    start=(kc == 0), stop=(kc == nk - 1)),
                          waits=w, inc=TE)
                        te_av[kc] = TE.n

                    readies = {}
                    for kc in (0, 1, 2, 3, 4, 5):
                        if kc < nk:
                            readies[kc] = emit_S(kc)
                    for pr in range(npair):
                        emit_exp(pr, [readies.pop(2 * pr),
                                      readies.pop(2 * pr + 1)])
                        for kc in (2 * pr + 6, 2 * pr + 7):
                            if kc < nk:
                                readies[kc] = emit_S(kc)
                        emit_AV(2 * pr)
                        emit_AV(2 * pr + 1)
                    te_y = TE.n
                    yst = yst2[g % 2]
                    E("vector", lambda v, yst=yst, yb=ybank: v.tensor_copy(
                        yst[0:65, :], yb[0:65, :]),
                      waits=[(TE, te_y)] + guard(("yst", g % 2)), inc=VE)
                    set_guard(("psY", g % 2), (VE, VE.n))
                    ve_y = VE.n
                    STy = dma_ctr(f"sty{g % 2}", store=True)
                    E("gpsimd", lambda gp, b=b, h=h, g=g, yst=yst: gp.dma_start(
                        out=y_d[b, h, :, g * 512:(g + 1) * 512], in_=yst[0:65, :]),
                      waits=[(VE, ve_y)], inc=STy, k=16)
                    set_guard(("yst", g % 2), (STy, STy.n))
                set_guard(("bqkv", bi), (TE, TE.n))
                set_guard(("vtok", bi), (TE, TE.n))
                yield
            for i in range(4):
                set_guard(("psA", i), (SE, SE.n), (VE, VE.n))
            set_guard(("psT", 4), (SE, SE.n))
            set_guard(("psT", 5), (SE, SE.n))
            set_guard("ps7", *guard(("psY", 1)))

        def b_stores():
            return [(c, c.n) for c in (dma_ctr("sty0"), dma_ctr("sty1"))]

        def phase_F1_gen(b, st_after_B):
            fbanks = [ps[6], ps[7]]
            for t in range(NT):
                xt, yl = fxt2[t % 2], fyTl2[t % 2]
                o1, lt = fo1b2[t % 2], flnt2[t % 2]
                LDx = dma_ctr(f"fldx{t % 2}")
                LDy = dma_ctr(f"fldy{t % 2}")
                E("sync", lambda s, b=b, t=t, xt=xt: s.dma_start(
                    out=xt[:, :], in_=x_d[b, t * 128:(t + 1) * 128, :]),
                  waits=guard(("fxt", t % 2)), inc=LDx, k=16)
                ld_x = (LDx, LDx.n)
                ylw = guard(("fyTl", t % 2))
                if t == 0:
                    ylw = ylw + st_after_B
                E("sync", lambda s, b=b, t=t, yl=yl: s.dma_start(
                    out=yl.rearrange("p (h c) -> p h c", h=16),
                    in_=y_d.rearrange("b h r c -> b r h c")[
                        b, :, :, t * 128:(t + 1) * 128]),
                  waits=ylw, inc=LDy, k=16)
                ld_y = (LDy, LDy.n)
                o1w = guard(("fo1b", t % 2))

                def emit_grp_stt(grp):
                    bank = fbanks[grp % 2]
                    rb = 16 + 4 * (grp % 2)
                    for j in range(4):
                        h = grp * 4 + j
                        E("vector", lambda v, o1=o1, xt=xt, h=h, j=j, bank=bank,
                          rb=rb:
                          v.scalar_tensor_tensor(
                              o1[:, h * 64:(h + 1) * 64],
                              bank[:, j * 65:j * 65 + 64],
                              sm[:, rb + j:rb + j + 1],
                              xt[:, h * 64:(h + 1) * 64], ALU.mult, ALU.add),
                          waits=([ld_x] + (o1w if h == 0 else [])
                                 if j == 0 else ()), inc=VE)
                    set_guard(("fps", grp % 2), (VE, VE.n))

                for grp in range(4):
                    bank = fbanks[grp % 2]
                    rb = 16 + 4 * (grp % 2)
                    for j in range(4):
                        h = grp * 4 + j
                        gw = []
                        if j == 0:
                            gw = [ld_y] + guard(("fps", grp % 2))
                            if grp % 2 == 0:
                                gw += guard(("psY", 0))
                            else:
                                gw += guard(("psY", 1)) + guard("ps7")
                        E("tensor", lambda te, yl=yl, h=h, j=j, bank=bank:
                          te.transpose(bank[:, j * 65:j * 65 + 65],
                                       yl[:, h * 128:(h + 1) * 128],
                                       ident[0:65, 0:65]),
                          waits=gw, inc=(TE if j == 3 else None))
                    te_tr = TE.n
                    if grp % 2 == 0:
                        set_guard(("psY", 0), (TE, te_tr))
                    else:
                        set_guard(("psY", 1), (TE, te_tr))
                        set_guard("ps7", (TE, te_tr))
                    E("vector", lambda v, bank=bank, rb=rb: v.reciprocal(
                        sm[:, rb:rb + 4],
                        bank[:, 0:260].rearrange("p (j c) -> p j c", c=65)[
                            :, :, 64:65]),
                      waits=[(TE, te_tr)], inc=VE)
                    vdrain()
                    if grp >= 1:
                        emit_grp_stt(grp - 1)
                emit_grp_stt(3)
                ve_o1 = VE.n
                set_guard(("fxt", t % 2), (VE, ve_o1))
                set_guard(("fyTl", t % 2), (TE, TE.n))
                set_guard(("fps", 0), *(guard(("fps", 0)) + [(VE, ve_o1)]))
                set_guard(("fps", 1), *(guard(("fps", 1)) + [(VE, ve_o1)]))
                set_guard(("psY", 0), *(guard(("psY", 0)) + [(VE, ve_o1)]))
                set_guard(("psY", 1), *(guard(("psY", 1)) + [(VE, ve_o1)]))
                set_guard("ps7", *(guard("ps7") + [(VE, ve_o1)]))
                STo = dma_ctr(f"fsto{t % 2}", store=True)
                STl = dma_ctr(f"fstl{t % 2}", store=True)
                E("gpsimd", lambda g, b=b, t=t, o1=o1: g.dma_start(
                    out=out1_d[b, t * 128:(t + 1) * 128, :], in_=o1[:, :]),
                  waits=[(VE, ve_o1)], inc=STo, k=16)
                st_o1 = (STo, STo.n)
                ve_ln = ln_tile(o1, lt, t, [(VE, ve_o1)] + guard(("flnt", t % 2)),
                                so=64, tb0=24, sqb=fsq)
                set_guard(("fo1b", t % 2), st_o1, (SE, SE.n), (VE, ve_ln))
                xmax.reduce(lt[:, :])
                E("gpsimd", lambda g, b=b, t=t, lt=lt: g.dma_start(
                    out=ln2_d[b, t * 128:(t + 1) * 128, :], in_=lt[:, :]),
                  waits=[(VE, VE.n)], inc=STl, k=16)
                set_guard(("flnt", t % 2), (STl, STl.n))
                yield
            xmax.finish()
            fin = ([(VE, VE.n), (SE, SE.n), (TE, TE.n)]
                   + [(c, c.n) for c in (dma_ctr("fsto0"), dma_ctr("fsto1"),
                                         dma_ctr("fstl0"), dma_ctr("fstl1"))])
            for key in [("bqkv", 0), ("bqkv", 1), ("vtok", 0), ("vtok", 1),
                        ("psY", 0), ("psY", 1), "ps7"]:
                set_guard(key, *(guard(key) + fin))

        def f1_stores():
            return [(c, c.n) for c in (dma_ctr("fsto0"), dma_ctr("fsto1"),
                                       dma_ctr("fstl0"), dma_ctr("fstl1"))]

        xmax = MaxAccum(12, 30)
        hmax = MaxAccum(13, 14)

        def collective_max(col, sig_col, rq_col, s_col):
            STcc = dma_ctr("stcc", store=True)
            part_max(col, 6)
            ve_m = VE.n
            E("gpsimd", lambda g: g.dma_start(out=cci_d[:, :], in_=sc[0:1, 6:7]),
              waits=[(VE, ve_m)], inc=STcc, k=16)
            st_in = STcc.n
            E("gpsimd", lambda g: g.collective_compute(
                "AllReduce", ALU.max, replica_groups=[list(range(NCORES))],
                ins=[cci_d[:, :].opt()], outs=[cco_d[:, :].opt()]),
              waits=[(STcc, st_in)], inc=CC)
            cc_done = CC.n
            E("gpsimd", lambda g: g.dma_start(out=sc[0:1, 7:8], in_=cco_d[:, :]),
              waits=[(CC, cc_done)], inc=STcc, k=16)
            st_out = (STcc, STcc.n)
            E("vector", lambda v, sig_col=sig_col: v.tensor_tensor(
                sc[0:1, 8:9], sc[0:1, 7:8], sc[0:1, sig_col:sig_col + 1],
                ALU.mult), waits=[st_out, (SE, se_sig)], inc=VE)
            vdrain()
            E("vector", lambda v: v.reciprocal(sc[0:1, 9:10], sc[0:1, 8:9]), inc=VE)
            vdrain()
            E("vector", lambda v: v.tensor_scalar(
                sc[0:1, 9:10], sc[0:1, 9:10], QMAX, None, ALU.mult), inc=VE)
            E("vector", lambda v: v.tensor_scalar(
                sc[0:1, 8:9], sc[0:1, 8:9], 1.0 / QMAX, None, ALU.mult), inc=VE)
            vdrain()
            ve_rq = VE.n
            bcast_col(sc[0:1, 9:10], rq_col, [(VE, ve_rq)])
            return bcast_col(sc[0:1, 8:9], s_col, [])

        def wquant_gen(wn_dram, nch_n, nfree_n, scol, codes_dst, wnt, ci8_sb,
                       cf_sb, first_fence, tbank0=4):
            nch_t = nfree_n // 128
            ncol = nch_n * 128
            for cc in range(nch_n):
                buf = wnt[cc % 2]
                LDwq = dma_ctr(f"ldwq_{buf.tensor.name}")
                lw = guard(("wqb", buf.tensor.name))
                if cc == 0 and first_fence:
                    lw = lw + first_fence
                E("sync", lambda s, cc=cc, buf=buf, nf=nfree_n, wn=wn_dram:
                  s.dma_start(out=buf[:, 0:nf],
                              in_=wn[cc * 128:(cc + 1) * 128, :]),
                  waits=lw, inc=LDwq, k=16)
                ld_w = (LDwq, LDwq.n)
                E("vector", lambda v, cc=cc, buf=buf, nf=nfree_n:
                  v.tensor_reduce(scl[:, scol + cc:scol + cc + 1],
                                  buf[:, 0:nf], AX.X, ALU.max,
                                  apply_absolute_value=True),
                  waits=[ld_w], inc=VE)
                vdrain()
                E("vector", lambda v, cc=cc: v.tensor_scalar(
                    scl[:, scol + cc:scol + cc + 1],
                    scl[:, scol + cc:scol + cc + 1], 1.0 / QMAX, None,
                    ALU.mult), inc=VE)
                vdrain()
                E("vector", lambda v, cc=cc: v.reciprocal(
                    scl[:, 104:105], scl[:, scol + cc:scol + cc + 1]), inc=VE)
                vdrain()
                E("vector", lambda v, cc=cc, buf=buf, nf=nfree_n, c8=ci8_sb:
                  v.tensor_scalar(c8[:, 0:nf], buf[:, 0:nf], scl[:, 104:105],
                                  None, ALU.mult),
                  waits=guard(("wq8", ci8_sb.tensor.name)), inc=VE)
                set_guard(("wqb", buf.tensor.name), (VE, VE.n))
                E("vector", lambda v, c8=ci8_sb, cf=cf_sb, nf=nfree_n:
                  v.tensor_copy(cf[:, 0:nf], c8[:, 0:nf]),
                  waits=guard(("wqf", cf_sb.tensor.name)), inc=VE)
                ve_cf = VE.n
                set_guard(("wq8", ci8_sb.tensor.name), (VE, VE.n))
                for j4 in range(0, nch_t, 4):
                    nj = min(4, nch_t - j4)
                    bank = ps[tbank0 + (j4 // 4) % 2]
                    for q in range(nj):
                        j = j4 + q
                        E("tensor", lambda te, cf=cf_sb, bank=bank, q=q, j=j:
                          te.transpose(bank[:, q * 128:(q + 1) * 128],
                                       cf[:, j * 128:(j + 1) * 128],
                                       ident[:, :]),
                          waits=([(VE, ve_cf)] +
                                 guard(("psT", tbank0 + (j4 // 4) % 2))
                                 if q == 0 else ()),
                          inc=(TE if q == nj - 1 else None))
                    te_tr = TE.n
                    E("vector", lambda v, bank=bank, cc=cc, j4=j4, nj=nj,
                      cd=codes_dst, ncol=ncol:
                      v.tensor_copy(
                          cd.rearrange("p (j o) -> p j o", o=ncol)[
                              :, j4:j4 + nj, cc * 128:(cc + 1) * 128],
                          bank[:, 0:nj * 128].rearrange(
                              "p (q w) -> p q w", q=nj)),
                      waits=[(TE, te_tr)], inc=VE)
                    set_guard(("psT", tbank0 + (j4 // 4) % 2), (VE, VE.n))
                set_guard(("wqf", cf_sb.tensor.name), (TE, TE.n))
                yield


        def drive(gens, pattern):
            alive = [iter(g) for g in gens]
            while alive:
                for gi, g in enumerate(list(alive)):
                    n = pattern[min(gi, len(pattern) - 1)]
                    for _ in range(n):
                        try:
                            next(g)
                        except StopIteration:
                            alive.remove(g)
                            break

        def a_values(b):
            return VE.n, [(c, c.n) for c in (dma_ctr("stg0"), dma_ctr("stg1"))]

        drive([phase_A_gen(0, None)], [99])
        ve_bias0, st_a0 = a_values(0)
        drive([phase_B_gen(0, ve_bias0, st_a0)], [99])
        st_b0 = b_stores()
        if INTERLEAVE_AF:
            drive([phase_A_gen(1, fence()), phase_F1_gen(0, st_b0)], [2, 1])
        else:
            drive([phase_F1_gen(0, st_b0)], [99])
            drive([phase_A_gen(1, fence())], [99])
        ve_bias1, st_a1 = a_values(1)
        drive([phase_B_gen(1, ve_bias1, st_a1)], [99])
        st_b1 = b_stores()
        wq_fence = fence()
        if INTERLEAVE_WQ:
            drive([phase_F1_gen(1, st_b1),
                   wquant_gen(wg_n_d, NIC, C, 0, wcg, wnt2, ci8, wq_cf,
                              wq_fence),
                   wquant_gen(wu_n_d, NIC, C, 22, wcu, wnt2, ci8, wq_cf,
                              None)],
                  [1, 2, 2])
        else:
            drive([phase_F1_gen(1, st_b1)], [99])
            drive([wquant_gen(wg_n_d, NIC, C, 0, wcg, wnt2, ci8, wq_cf,
                              fence())], [99])
            drive([wquant_gen(wu_n_d, NIC, C, 22, wcu, wnt2, ci8, wq_cf,
                              None)], [99])
        st_f1 = f1_stores()
        cc1 = collective_max(12, 2, 1, 3)
        vdrain()
        E("vector", lambda v: v.tensor_scalar(
            scl[:, 52:74], scl[:, 0:22], sm[:, 3:4], None, ALU.mult),
          waits=[(VE, cc1)], inc=VE)
        E("vector", lambda v: v.tensor_scalar(
            scl[:, 74:96], scl[:, 22:44], sm[:, 3:4], None, ALU.mult), inc=VE)
        ve_sxw = VE.n

        p2_fence = fence()
        for tt in range(2 * NT):
            b, t = divmod(tt, NT)
            lt = ln2l2[tt % 2]
            LDl = dma_ctr(f"ldl{tt % 2}")
            lw = guard(("p2ln", tt % 2))
            if tt == 0:
                lw = lw + st_f1 + p2_fence
            E("sync", lambda s, b=b, t=t, lt=lt: s.dma_start(
                out=lt[:, :], in_=ln2_d[b, t * 128:(t + 1) * 128, :]),
              waits=lw, inc=LDl, k=16)
            ld_ln = (LDl, LDl.n)
            E("vector", lambda v, lt=lt: v.tensor_scalar(
                xc8[:, :], lt[:, :], sm[:, 1:2], None, ALU.mult),
              waits=[ld_ln, (VE, cc1)] + guard("xc8"), inc=VE)
            E("vector", lambda v: v.tensor_copy(xcf[:, :], xc8[:, :]),
              waits=guard("xcf"), inc=VE)
            ve_xcb = VE.n
            set_guard(("p2ln", tt % 2), (VE, ve_xcb))
            set_guard("xc8", (VE, ve_xcb))
            for half in range(2):
                bank = ps[4 + half]
                for q in range(4):
                    cchunk = half * 4 + q
                    tw = []
                    if q == 0:
                        tw = [(VE, ve_xcb)] + guard(("psT", 4 + half))
                    E("tensor", lambda te, q=q, c=cchunk, bank=bank:
                      te.transpose(bank[:, q * 128:(q + 1) * 128],
                                   xcf[:, c * 128:(c + 1) * 128], ident[:, :]),
                      waits=tw, inc=(TE if q == 3 else None))
                te_tr = TE.n
                E("vector", lambda v, tt=tt, half=half, bank=bank: v.tensor_copy(
                    xcT.rearrange("p (c w) -> p c w", c=8)[
                        :, half * 4:(half + 1) * 4, tt * 128:(tt + 1) * 128],
                    bank.rearrange("p (q w) -> p q w", q=4)),
                  waits=[(TE, te_tr)], inc=VE)
                set_guard(("psT", 4 + half), (VE, VE.n))
            set_guard("xcf", (TE, TE.n))
        ve_xcT = VE.n

        for tg in range(TG_ALL):
            for oc in range(NIC):
                psG = ps[oc % 2]
                psU = ps[2 + oc % 2]
                gkey = ("psA", oc % 2)
                ukey = ("psA", 2 + oc % 2)
                hb = hb3[(tg * NIC + oc) % 3]
                sg = sgb2[oc % 2]
                mm_w = [(VE, ve_xcT), (VE, ve_sxw)] + guard(gkey)
                for cc in range(NCC):
                    E("tensor", lambda te, psG=psG, oc=oc, cc=cc, tg=tg:
                      te.matmul(psG[:, :],
                                wcg[:, cc * IFF + oc * 128:
                                    cc * IFF + oc * 128 + 128],
                                xcT[:, cc * 4096 + tg * 512:
                                    cc * 4096 + tg * 512 + 512],
                                start=(cc == 0), stop=(cc == NCC - 1)),
                      waits=mm_w if cc == 0 else (),
                      inc=(TE if cc == NCC - 1 else None))
                te_g = TE.n
                mm_w2 = guard(ukey)
                for cc in range(NCC):
                    E("tensor", lambda te, psU=psU, oc=oc, cc=cc, tg=tg:
                      te.matmul(psU[:, :],
                                wcu[:, cc * IFF + oc * 128:
                                    cc * IFF + oc * 128 + 128],
                                xcT[:, cc * 4096 + tg * 512:
                                    cc * 4096 + tg * 512 + 512],
                                start=(cc == 0), stop=(cc == NCC - 1)),
                      waits=mm_w2 if cc == 0 else (),
                      inc=(TE if cc == NCC - 1 else None))
                te_u = TE.n
                E("scalar", lambda s, psG=psG, sg=sg, oc=oc: s.activation(
                    sg[:, :], psG[:, :], AF.Silu,
                    scale=scl[:, 52 + oc:53 + oc]),
                  waits=[(TE, te_g)] + guard(("sgb", oc % 2)), inc=SE)
                set_guard(gkey, (SE, SE.n))
                se_sg = SE.n
                E("vector", lambda v, psU=psU, sg=sg, hb=hb, oc=oc:
                  v.scalar_tensor_tensor(hb[:, :], psU[:, :],
                                         scl[:, 74 + oc:75 + oc], sg[:, :],
                                         ALU.mult, ALU.mult),
                  waits=[(TE, te_u), (SE, se_sg)] +
                        guard(("hb", (tg * NIC + oc) % 3)), inc=VE)
                set_guard(ukey, (VE, VE.n))
                set_guard(("sgb", oc % 2), (VE, VE.n))
                hmax.reduce(hb[:, :])
                ve_h = VE.n
                STh = dma_ctr(f"sth{(tg * NIC + oc) % 3}", store=True)
                E("gpsimd", lambda g, oc=oc, tg=tg, hb=hb: g.dma_start(
                    out=h_d[oc * 128:(oc + 1) * 128, tg * 512:(tg + 1) * 512],
                    in_=hb[:, :]), waits=[(VE, ve_h)], inc=STh, k=16)
                set_guard(("hb", (tg * NIC + oc) % 3), (STh, STh.n))
        st_p2 = [(c, c.n) for c in (dma_ctr("sth0"), dma_ctr("sth1"),
                                    dma_ctr("sth2"))]
        te_p2 = TE.n

        if debug:
            for name, src in [("d_xcT", xcT), ("d_wcg", wcg), ("d_wcu", wcu),
                              ("d_scl", scl), ("d_sm", sm), ("d_sc", sc)]:
                shp = list(src.shape)
                dt = BF16 if name in ("d_xcT", "d_wcg", "d_wcu") else F32
                dd = nc.dram_tensor(name, shp, dt, kind="ExternalOutput").ap()
                STd = dma_ctr("stdbg", store=True)
                E("gpsimd", lambda g, dd=dd, src=src: g.dma_start(
                    out=dd, in_=src), waits=fence(), inc=STd, k=16)

        drive([wquant_gen(wd_n_d, NCC, IFF, 44, wcd, wnt3, ci83, wq_cf3,
                          [(TE, te_p2)] + all_stores())], [99])
        wq_d = VE.n
        hmax.finish()
        cc2 = collective_max(13, 3, 2, 4)
        vdrain()
        E("vector", lambda v: v.tensor_scalar(
            scl[:, 96:104], scl[:, 44:52], sm[:, 4:5], None, ALU.mult),
          waits=[(VE, cc2)], inc=VE)
        ve_sdh = VE.n

        def p3_quant(tg):
            b = tg // 4
            nhb = nhb2[tg % 2]
            for ic in range(NIC):
                hl = hl2[ic % 2]
                LDh = dma_ctr(f"ldh{ic % 2}")
                lw = guard(("hl", ic % 2))
                if tg == 0 and ic == 0:
                    lw = lw + st_p2
                E("sync", lambda s, ic=ic, tg=tg, hl=hl: s.dma_start(
                    out=hl[:, :],
                    in_=h_d[ic * 128:(ic + 1) * 128, tg * 512:(tg + 1) * 512]),
                  waits=lw, inc=LDh, k=16)
                ld_h = (LDh, LDh.n)
                qw = [ld_h, (VE, cc2)] + guard("ci83")
                if ic == 0:
                    qw += guard(("nhb", tg % 2))
                E("vector", lambda v, hl=hl: v.tensor_scalar(
                    ci83[:, 0:512], hl[:, :], sm[:, 2:3], None, ALU.mult),
                  waits=qw, inc=VE)
                E("vector", lambda v, ic=ic, nhb=nhb: v.tensor_copy(
                    nhb[:, ic * 512:(ic + 1) * 512], ci83[:, 0:512]), inc=VE)
                set_guard(("hl", ic % 2), (VE, VE.n))
                set_guard("ci83", (VE, VE.n))
            return nhb, VE.n

        def p3_mmpost(tg, nhb, ve_nh):
            b = tg // 4
            for q in range(4):
                t_loc = (tg % 4) * 4 + q
                LDo1 = dma_ctr(f"ldo1c{q}")
                E("gpsimd", lambda s, b=b, t_loc=t_loc, o1c=o1c4[q]:
                  s.dma_start(
                      out=o1c[:, :],
                      in_=out1_d[b, t_loc * 128:(t_loc + 1) * 128, :]),
                  waits=guard(("o1c", q)), inc=LDo1, k=16)
                set_guard(("o1cld", q), (LDo1, LDo1.n))

            se_dsc = [0] * NCC

            def emit_mms(fc):
                psD = ps[fc % 2]
                mm_w = [(VE, ve_nh), (VE, wq_d)] + guard(("psD", fc % 2))
                for ic in range(NIC):
                    E("tensor", lambda te, psD=psD, fc=fc, ic=ic, nhb=nhb:
                      te.matmul(psD[:, :],
                                wcd[:, ic * 1024 + fc * 128:
                                    ic * 1024 + fc * 128 + 128],
                                nhb[:, ic * 512:(ic + 1) * 512],
                                start=(ic == 0), stop=(ic == NIC - 1)),
                      waits=mm_w if ic == 0 else (),
                      inc=(TE if ic == NIC - 1 else None))
                te_d = TE.n
                E("scalar", lambda s, psD=psD, dsc=dsc2[fc % 2], fc=fc:
                  s.activation(dsc[:, :], psD[:, :], AF.Identity,
                               scale=scl[:, 96 + fc:97 + fc]),
                  waits=[(TE, te_d), (VE, ve_sdh)] + guard(("dsc", fc % 2)),
                  inc=SE)
                set_guard(("psD", fc % 2), (SE, SE.n))
                se_dsc[fc] = SE.n

            def emit_post(fc):
                dsc = dsc2[fc % 2]
                bank = ps[4 + fc % 2]
                for q in range(4):
                    E("tensor", lambda te, dsc=dsc, bank=bank, q=q:
                      te.transpose(bank[:, q * 128:(q + 1) * 128],
                                   dsc[:, q * 128:(q + 1) * 128], ident[:, :]),
                      waits=([(SE, se_dsc[fc])] + guard(("psT", 4 + fc % 2))
                             if q == 0 else ()),
                      inc=(TE if q == 3 else None))
                set_guard(("dsc", fc % 2), (TE, TE.n))
                te_tr = TE.n
                for q in range(4):
                    t_loc = (tg % 4) * 4 + q
                    o1c = o1c4[q]
                    o2 = o2b4[q]
                    sw = [(TE, te_tr)] + guard(("o1cld", q))
                    if fc == 0:
                        sw += guard(("o2b", q))
                    E("vector", lambda v, o2=o2, o1c=o1c, bank=bank, q=q, fc=fc:
                      v.scalar_tensor_tensor(
                          o2[:, fc * 128:(fc + 1) * 128],
                          bank[:, q * 128:(q + 1) * 128], 1.0,
                          o1c[:, fc * 128:(fc + 1) * 128], ALU.mult, ALU.add),
                      waits=sw, inc=VE)
                    if fc == NCC - 1:
                        set_guard(("o1c", q), (VE, VE.n))
                        ve_o2 = VE.n
                        STo2 = dma_ctr(f"sto2_{q}", store=True)
                        E("gpsimd", lambda g, b=b, t_loc=t_loc, o2=o2:
                          g.dma_start(
                              out=out_d[b, t_loc * 128:(t_loc + 1) * 128, :],
                              in_=o2[:, :]), waits=[(VE, ve_o2)], inc=STo2, k=16)
                        set_guard(("o2b", q), (STo2, STo2.n))
                set_guard(("psT", 4 + fc % 2), (VE, VE.n))

            for fc in range(NCC):
                emit_mms(fc)
                if fc >= 1:
                    emit_post(fc - 1)
            emit_post(NCC - 1)
            set_guard(("nhb", tg % 2), (TE, TE.n))
            if debug and tg == 2:
                dnh = nc.dram_tensor("d_nh2", [128, NIC * 512], BF16,
                                     kind="ExternalOutput").ap()
                STd = dma_ctr("stdbg", store=True)
                E("gpsimd", lambda g, dnh=dnh, nhb=nhb: g.dma_start(
                    out=dnh, in_=nhb), waits=[(VE, ve_nh)], inc=STd, k=16)
                set_guard(("nhb", tg % 2), (TE, TE.n), (STd, STd.n))

        pending = None
        for tg in range(TG_ALL):
            state = p3_quant(tg)
            if pending is not None:
                p3_mmpost(*pending)
            pending = (tg, state[0], state[1])
        p3_mmpost(*pending)

        if debug:
            STd = dma_ctr("stdbg", store=True)
            dsm = nc.dram_tensor("d_sm_end", [128, 32], F32,
                                 kind="ExternalOutput").ap()
            E("gpsimd", lambda g: g.dma_start(out=dsm, in_=sm),
              waits=fence(), inc=STd, k=16)
            for name, src in [("d_qkvT", qkvT_d), ("d_y", y_d),
                              ("d_out1", out1_d), ("d_ln2", ln2_d),
                              ("d_h", h_d)]:
                dst = nc.dram_tensor(name, list(src.shape), F32,
                                     kind="ExternalOutput").ap()
                E("gpsimd", lambda g, dst=dst, src=src: g.dma_start(
                    out=dst, in_=src), waits=fence(), inc=STd, k=16)

        for c, v in all_stores():
            if v > 0:
                E("gpsimd", lambda g, c=c, v=v: g.wait_ge(c.sem, v))

        per_engine = {"sync": [], "gpsimd": [], "tensor": [], "vector": [],
                      "scalar": []}
        for op in ops:
            per_engine[op.engine].append(op)
        produced_by = {TE.sem: "tensor", VE.sem: "vector", SE.sem: "scalar"}

        def make_body(eng_name, eng_ops):
            def body(eng):
                seen = {}
                for op in eng_ops:
                    for (sem, val, _nm) in op.waits:
                        if produced_by.get(sem) == eng_name:
                            continue
                        if seen.get(sem, -1) >= val:
                            continue
                        eng.wait_ge(sem, val)
                        seen[sem] = val
                    bi = op.emit(eng)
                    for (sem, k) in op.incs:
                        bi.then_inc(sem, k)
            return body

        with nc.Block() as block:
            block.sync(make_body("sync", per_engine["sync"]))
            block.gpsimd(make_body("gpsimd", per_engine["gpsimd"]))
            block.tensor(make_body("tensor", per_engine["tensor"]))
            block.vector(make_body("vector", per_engine["vector"]))
            block.scalar(make_body("scalar", per_engine["scalar"]))

    return nc


def _host_inputs(L0_attn_x, L0_attn_W, L0_attn_b, L0_ffn_clip_x,
                 L0_ffn_clip_down, L0_ffn_w_gate, L0_ffn_w_up, L0_ffn_w_down):
    import ml_dtypes
    x = np.ascontiguousarray(np.asarray(L0_attn_x, dtype=np.float32))
    W = np.asarray(L0_attn_W, dtype=np.float32)
    b = np.asarray(L0_attn_b, dtype=np.float32)
    wg = np.asarray(L0_ffn_w_gate, dtype=np.float32)
    wu = np.asarray(L0_ffn_w_up, dtype=np.float32)
    wd = np.asarray(L0_ffn_w_down, dtype=np.float32)
    u = np.arange(896)[None, :]
    p = np.arange(128)[:, None]
    common = {
        "wt": np.ascontiguousarray(W.T),
        "bias_sb": np.ascontiguousarray(b.reshape(24, 128).T),
        "bias_row": np.ascontiguousarray(b.reshape(1, 3072)),
        "wg_n": np.ascontiguousarray(wg),
        "wu_n": np.ascontiguousarray(wu),
        "wd_n": np.ascontiguousarray(wd),
        "wg_t": np.ascontiguousarray(wg.T),
        "wu_t": np.ascontiguousarray(wu.T),
        "wd_t": np.ascontiguousarray(wd.T),
        "clip2": np.array([[np.asarray(L0_ffn_clip_x).reshape(-1)[0],
                            np.asarray(L0_ffn_clip_down).reshape(-1)[0]]],
                          dtype=np.float32),
        "ident": np.eye(128, dtype=np.float32),
        "identb": np.eye(128, dtype=np.float32).astype(ml_dtypes.bfloat16),
        "ones1": np.ones((1, 128), dtype=np.float32),
        "maskbig": np.where(u - p >= 384, np.float32(0.0),
                            np.float32(MASK_NEG)).astype(np.float32),
    }
    return [dict(common, x=x[2 * i:2 * i + 2]) for i in range(NCORES)]


_NC_CACHE = {}


def kernel(**inputs):
    in_maps = _host_inputs(**inputs)
    if "nc" not in _NC_CACHE:
        _NC_CACHE["nc"] = build_program()
    nc = _NC_CACHE["nc"]
    res = run_bass_kernel_spmd(nc, in_maps, core_ids=list(range(NCORES)))
    out = np.concatenate([np.asarray(r["out"]) for r in res.results], axis=0)
    return np.ascontiguousarray(out.astype(np.float32))


# version 11
# speedup vs baseline: 1.0585x; 1.0585x over previous
import contextlib
import numpy as np

import concourse.bass as bass
import concourse.mybir as mybir
from concourse.bass_utils import run_bass_kernel_spmd

F32 = mybir.dt.float32
BF16 = mybir.dt.bfloat16
I8 = mybir.dt.int8
AF = mybir.ActivationFunctionType
ALU = mybir.AluOpType
AX = mybir.AxisListType

NCORES = 8
T = 2048
C = 1024
NH = 16
HD = 64
IFF = 2816
NT = T // 128
NCC = C // 128
NIC = IFF // 128
TG_ALL = 8
MASK_NEG = -1.0e5
EPS = 1e-5
QMAX = 127.0


class Op:
    __slots__ = ("engine", "waits", "emit", "incs")

    def __init__(self, engine, waits, emit, incs):
        self.engine = engine
        self.waits = waits
        self.emit = emit
        self.incs = incs


def build_program(debug=False, INTERLEAVE_AF=True, INTERLEAVE_WQ=True):
    nc = bass.Bass("TRN2", target_bir_lowering=False)

    x_d = nc.dram_tensor("x", [2, T, C], F32, kind="ExternalInput").ap()
    wt_d = nc.dram_tensor("wt", [C, 3 * C], F32, kind="ExternalInput").ap()
    bias_d = nc.dram_tensor("bias_sb", [128, 24], F32, kind="ExternalInput").ap()
    wg_n_d = nc.dram_tensor("wg_n", [IFF, C], F32, kind="ExternalInput").ap()
    wu_n_d = nc.dram_tensor("wu_n", [IFF, C], F32, kind="ExternalInput").ap()
    wd_n_d = nc.dram_tensor("wd_n", [C, IFF], F32, kind="ExternalInput").ap()
    wg_t_d = nc.dram_tensor("wg_t", [C, IFF], F32, kind="ExternalInput").ap()
    wu_t_d = nc.dram_tensor("wu_t", [C, IFF], F32, kind="ExternalInput").ap()
    wd_t_d = nc.dram_tensor("wd_t", [IFF, C], F32, kind="ExternalInput").ap()
    clip_d = nc.dram_tensor("clip2", [1, 2], F32, kind="ExternalInput").ap()
    ident_d = nc.dram_tensor("ident", [128, 128], F32, kind="ExternalInput").ap()
    identb_d = nc.dram_tensor("identb", [128, 128], BF16, kind="ExternalInput").ap()
    mask_d = nc.dram_tensor("maskbig", [128, 896], F32, kind="ExternalInput").ap()
    ones_d = nc.dram_tensor("ones1", [1, 128], F32, kind="ExternalInput").ap()
    out_d = nc.dram_tensor("out", [2, T, C], F32, kind="ExternalOutput").ap()

    qkvT_d = nc.dram_tensor("qkvT_s", [2, 3 * C, T], F32).ap()
    y_d = nc.dram_tensor("y_s", [2, NH, HD + 1, T], F32).ap()
    out1_d = nc.dram_tensor("out1_s", [2, T, C], F32).ap()
    ln2_d = nc.dram_tensor("ln2_s", [2, T, C], F32).ap()
    h_d = nc.dram_tensor("h_s", [IFF, 2 * T], F32).ap()
    srow_d = nc.dram_tensor("srow_s", [IFF], F32).ap()
    cci_d = nc.dram_tensor("cc_in", [1, 1], F32).ap()
    cco_d = nc.dram_tensor("cc_out", [1, 1], F32).ap()

    SB_BASE = 16544

    def sb(name, shape, dtype, off):
        return nc.alloc_sbuf_tensor_at(name, list(shape), dtype,
                                       offset=SB_BASE + off).ap()

    ident = sb("c_ident", [128, 128], F32, 0)
    maskbig = sb("c_mask", [128, 896], F32, 512)
    bias_sb = sb("c_bias", [128, 24], F32, 4096)
    ones1 = sb("c_ones", [1, 128], F32, 4224)
    identb = sb("c_identb", [128, 128], BF16, 4736)
    stats = sb("c_stats", [128, 128], F32, 4992)
    sm = sb("c_small", [128, 32], F32, 5504)
    sc = sb("c_scal", [1, 16], F32, 5632)
    scl = sb("c_scl", [128, 112], F32, 5696)

    wtpf = sb("a_wtpf", [128, 2 * 1024], F32, 6144)
    whi = sb("a_whi", [128, 8 * 1024], BF16, 14336)
    wlo = sb("a_wlo", [128, 8 * 1024], BF16, 30720)
    ln1T_hi = sb("a_ln1Th", [128, 8 * 2048], BF16, 47104)
    ln1T_lo = sb("a_ln1Tl", [128, 8 * 2048], BF16, 79872)
    xt2 = [sb(f"a_xt{i}", [128, 1024], F32, 112640 + 4096 * i) for i in range(2)]
    lnt2 = [sb(f"a_lnt{i}", [128, 1024], F32, 120832 + 4096 * i) for i in range(2)]
    sqjunk_a = sb("a_sqj", [128, 1024], F32, 129024)
    stg2 = [sb(f"a_stg{i}", [128, 1024], F32, 133120 + 4096 * i) for i in range(2)]

    qb2 = [sb(f"b_q{i}", [64, 2048], F32, 141312 + 8192 * i) for i in range(2)]
    kb2 = [sb(f"b_k{i}", [64, 2048], F32, 157696 + 8192 * i) for i in range(2)]
    vb2 = [sb(f"b_v{i}", [64, 2048], F32, 174080 + 8192 * i) for i in range(2)]
    vtok2 = [sb(f"b_vt{i}", [128, 16 * 65], F32, 190464 + 4160 * i) for i in range(2)]
    pbD = [sb(f"b_p{i}", [128, 1024], F32, 198784 + 4096 * i) for i in range(2)]
    yst2 = [sb(f"b_yst{i}", [128, 512], F32, 206976 + 2048 * i) for i in range(2)]

    fyTl2 = [sb(f"f_yTl{i}", [65, 16 * 128], F32, 141312 + 8192 * i)
             for i in range(2)]
    fo1b2 = [sb(f"f_o1{i}", [128, 1024], F32, 157696 + 4096 * i)
             for i in range(2)]
    fsq = sb("f_sq", [128, 1024], F32, 174080)
    fxt2 = [sb(f"f_xt{i}", [128, 1024], F32, 178176 + 4096 * i)
            for i in range(2)]
    flnt2 = [sb(f"f_lnt{i}", [128, 1024], F32, 186368 + 4096 * i)
             for i in range(2)]

    wcg = sb("p_wcg", [128, 8 * IFF], BF16, 6144)
    wcu = sb("p_wcu", [128, 8 * IFF], BF16, 51200)
    xcT = sb("p_xcT", [128, 8 * 4096], BF16, 96256)
    hb3 = [sb(f"p_hb{i}", [128, 512], F32, 161792 + 2048 * i) for i in range(3)]
    sgb2 = [sb(f"p_sg{i}", [128, 512], F32, 167936 + 2048 * i) for i in range(2)]
    wnt2 = [sb(f"p_wnt{i}", [128, IFF], F32, 96256 + 11264 * i) for i in range(2)]
    wq_cf = sb("p_wqcf", [128, 1024], F32, 118784)
    ci8 = sb("p_ci8", [128, 1024], I8, 122880)
    xc8 = sb("p_xc8", [128, 1024], I8, 180224)
    xcf = sb("p_xcf", [128, 1024], F32, 181248)
    ln2l2 = [sb(f"p_ln2l{i}", [128, 1024], F32, 172032 + 4096 * i) for i in range(2)]
    wcd = sb("p_wcd", [128, NIC * 1024], BF16, 96256)
    nhb2 = [sb(f"p_nhb{i}", [128, NIC * 512], BF16,
               6144 if i == 0 else 69632) for i in range(2)]
    dsc2 = [sb(f"p_dsc{i}", [128, 512], F32, 28672 + 2048 * i) for i in range(2)]
    o1c4 = [sb(f"p_o1c{i}", [128, 1024], F32, 32768 + 4096 * i) for i in range(4)]
    o2b4 = [sb(f"p_o2{i}", [128, 1024], F32, 49152 + 4096 * i) for i in range(4)]
    hl2 = [sb(f"p_hl{i}", [128, 512], F32, 65536 + 2048 * i) for i in range(2)]
    wnt3 = [sb(f"p_wnt3{i}", [128, IFF], F32, 180224 + 11264 * i) for i in range(2)]
    ci83 = sb("p_ci83", [128, IFF], I8, 206848)
    wq_cf3 = sb("p_wqcf3", [128, IFF], F32, 141312)

    psd0 = nc.alloc_psum_tensor("psd0", [128, 1024], F32).ap()
    psd1 = nc.alloc_psum_tensor("psd1", [128, 1024], F32).ap()
    psd2 = nc.alloc_psum_tensor("psd2", [128, 1024], F32).ap()
    _pss = [nc.alloc_psum_tensor(f"ps{i}", [128, 512], F32).ap()
            for i in range(6, 8)]
    ps = [psd0[:, 0:512], psd0[:, 512:1024], psd1[:, 0:512],
          psd1[:, 512:1024], psd2[:, 0:512], psd2[:, 512:1024]] + _pss
    psd = [psd0, psd1, psd2]
    ps7 = ps[7]

    ops = []

    with contextlib.ExitStack() as stack:
        class Ctr:
            def __init__(self, name):
                self.sem = stack.enter_context(nc.semaphore(name))
                self.n = 0
                self.name = name

        TE = Ctr("te")
        VE = Ctr("ve")
        SE = Ctr("se")
        CC = Ctr("cc")
        _dma_ctrs = {}
        _store_ctrs = []

        def dma_ctr(name, store=False):
            if name not in _dma_ctrs:
                c = Ctr(name)
                _dma_ctrs[name] = c
                if store:
                    _store_ctrs.append(c)
            return _dma_ctrs[name]

        guards = {}

        def guard(key):
            return list(guards.get(key, ()))

        def set_guard(key, *pairs):
            guards[key] = [(c, v) for (c, v) in pairs]

        def all_stores():
            return [(c, c.n) for c in _store_ctrs]

        def fence():
            return [(TE, TE.n), (VE, VE.n), (SE, SE.n)] + all_stores()

        def E(engine, emit, waits=(), inc=None, k=1):
            incs = []
            if inc is not None:
                inc.n += k
                incs = [(inc.sem, k)]
            ws = [(c.sem, v, c.name) for (c, v) in waits if v > 0]
            ops.append(Op(engine, ws, emit, incs))

        LDc = dma_ctr("ldc")
        E("sync", lambda s: s.dma_start(out=ident[:, :], in_=ident_d[:, :]), inc=LDc, k=16)
        E("sync", lambda s: s.dma_start(out=identb[:, :], in_=identb_d[:, :]), inc=LDc, k=16)
        E("sync", lambda s: s.dma_start(out=maskbig[:, :], in_=mask_d[:, :]), inc=LDc, k=16)
        E("sync", lambda s: s.dma_start(out=bias_sb[:, :], in_=bias_d[:, :]), inc=LDc, k=16)
        E("sync", lambda s: s.dma_start(out=ones1[:, :], in_=ones_d[:, :]), inc=LDc, k=16)
        E("sync", lambda s: s.dma_start(out=sc[0:1, 0:2], in_=clip_d[:, :]), inc=LDc, k=16)
        consts_ld = LDc.n
        E("scalar", lambda s: s.activation(sc[0:1, 2:4], sc[0:1, 0:2], AF.Sigmoid),
          waits=[(LDc, consts_ld)], inc=SE)
        se_sig = SE.n
        E("vector", lambda v: v.memset(sm[:, :], 0.0),
          waits=[(LDc, consts_ld)], inc=VE)
        E("vector", lambda v: v.memset(stats[:, :], 0.0), inc=VE)

        def vdrain():
            E("vector", lambda v: v.drain(), inc=VE)

        def ln_tile(xtile, lntile, tcol, extra_waits, so=0, tb0=6, sqb=None):
            tb = tb0 + 3 * (tcol % 2)
            sqjunk = sqb if sqb is not None else sqjunk_a
            tcol = tcol + so
            E("vector", lambda v, xt=xtile, c=tcol: v.tensor_reduce(
                stats[:, c:c + 1], xt[:, :], AX.X, ALU.add),
              waits=extra_waits, inc=VE)
            vdrain()
            E("vector", lambda v, c=tcol: v.tensor_scalar(
                stats[:, 32 + c:33 + c], stats[:, c:c + 1], -1.0 / C, None,
                ALU.mult), inc=VE)
            vdrain()
            E("vector", lambda v, xt=xtile, lt=lntile, c=tcol: v.tensor_scalar(
                lt[:, :], xt[:, :], stats[:, 32 + c:33 + c], None, ALU.add),
              inc=VE)
            ve_d = VE.n
            E("scalar", lambda s, lt=lntile, c=tcol: s.activation(
                sqjunk[:, :], lt[:, :], AF.Square,
                accum_out=stats[:, 16 + c:17 + c]),
              waits=[(VE, ve_d)], inc=SE)
            se_sq = SE.n
            E("vector", lambda v, c=tcol, tb=tb: v.tensor_scalar(
                sm[:, tb + 1:tb + 2], stats[:, 16 + c:17 + c], 1.0 / C, EPS,
                ALU.mult, ALU.add), waits=[(SE, se_sq)], inc=VE)
            ve_var = VE.n
            E("scalar", lambda s, tb=tb: s.activation(
                sm[:, tb + 2:tb + 3], sm[:, tb + 1:tb + 2], AF.Sqrt),
              waits=[(VE, ve_var)], inc=SE)
            se_sqrt = SE.n
            E("vector", lambda v, c=tcol, tb=tb: v.reciprocal(
                stats[:, 48 + c:49 + c], sm[:, tb + 2:tb + 3]),
              waits=[(SE, se_sqrt)], inc=VE)
            vdrain()
            E("vector", lambda v, c=tcol, tb=tb: v.tensor_scalar(
                sm[:, tb:tb + 1], stats[:, 48 + c:49 + c],
                stats[:, 48 + c:49 + c], None, ALU.mult), inc=VE)
            vdrain()
            E("vector", lambda v, tb=tb: v.tensor_tensor(
                sm[:, tb:tb + 1], sm[:, tb:tb + 1], sm[:, tb + 1:tb + 2],
                ALU.mult), inc=VE)
            vdrain()
            E("vector", lambda v, tb=tb: v.tensor_scalar(
                sm[:, tb:tb + 1], sm[:, tb:tb + 1], -0.5, 1.5,
                ALU.mult, ALU.add), inc=VE)
            vdrain()
            E("vector", lambda v, c=tcol, tb=tb: v.tensor_tensor(
                stats[:, 48 + c:49 + c], stats[:, 48 + c:49 + c],
                sm[:, tb:tb + 1], ALU.mult), inc=VE)
            vdrain()
            E("vector", lambda v, lt=lntile, c=tcol: v.tensor_scalar(
                lt[:, :], lt[:, :], stats[:, 48 + c:49 + c], None, ALU.mult),
              inc=VE)
            return VE.n

        class MaxAccum:

            def __init__(self, dst, stage0=14):
                self.dst = dst
                self.s0 = stage0
                self.i = 0
                self.pending = None

            def reduce(self, src_ap, waits=()):
                col = self.s0 + self.i % 2
                prev = self.pending
                self.pending = None
                E("vector", lambda v, src=src_ap, col=col: v.tensor_reduce(
                    sm[:, col:col + 1], src, AX.X, ALU.max,
                    apply_absolute_value=True), waits=list(waits), inc=VE)
                if prev is not None:
                    col2, d = prev, self.dst
                    E("vector", lambda v, col=col2, d=d: v.tensor_tensor(
                        sm[:, d:d + 1], sm[:, d:d + 1], sm[:, col:col + 1],
                        ALU.max), inc=VE)
                self.pending = col
                self.i += 1

            def flush_one(self):
                if self.pending is not None:
                    col, d = self.pending, self.dst
                    E("vector", lambda v, col=col, d=d: v.tensor_tensor(
                        sm[:, d:d + 1], sm[:, d:d + 1], sm[:, col:col + 1],
                        ALU.max), inc=VE)
                    self.pending = None

            def finish(self):
                vdrain()
                self.flush_one()
                return VE.n

        def bcast_col(src_ap, dst_col, extra_waits):
            E("tensor", lambda te, src=src_ap: te.matmul(
                ps7[:, 0:1], ones1[:, :], src, start=True, stop=True),
              waits=extra_waits + guard("ps7"), inc=TE)
            E("vector", lambda v, c=dst_col: v.tensor_copy(
                sm[:, c:c + 1], ps7[:, 0:1]), waits=[(TE, TE.n)], inc=VE)
            set_guard("ps7", (VE, VE.n))
            return VE.n

        def part_max(col, out_sc_col):
            vdrain()
            E("tensor", lambda te, c=col: te.transpose(
                ps7[0:1, 0:128], sm[:, c:c + 1], ident[:, :]),
              waits=[(VE, VE.n)] + guard("ps7"), inc=TE)
            E("vector", lambda v, oc=out_sc_col: v.tensor_reduce(
                sc[0:1, oc:oc + 1], ps7[0:1, 0:128], AX.X, ALU.max),
              waits=[(TE, TE.n)], inc=VE)
            set_guard("ps7", (VE, VE.n))
            return VE.n

        def emit_piece_loads(piece):
            lst = []
            for hp in range(4):
                LDw = dma_ctr(f"ldw{hp % 2}")
                E("sync", lambda s, piece=piece, hp=hp: s.dma_start(
                    out=wtpf.rearrange("p (c o) -> p c o", c=8),
                    in_=wt_d.rearrange("(c p) o -> p c o", p=128)[
                        :, :, piece * 1024 + hp * 256:
                        piece * 1024 + hp * 256 + 256]),
                  waits=guard(("wtpf",)), inc=LDw, k=16)
                ld_w = (LDw, LDw.n)
                E("vector", lambda v, hp=hp: v.tensor_copy(
                    whi.rearrange("p (c o) -> p c o", c=8)[
                        :, :, hp * 256:(hp + 1) * 256],
                    wtpf.rearrange("p (c o) -> p c o", c=8)),
                  waits=[ld_w] + guard(("wtp", hp)), inc=VE)
                E("vector", lambda v, hp=hp: v.scalar_tensor_tensor(
                    wlo.rearrange("p (c o) -> p c o", c=8)[
                        :, :, hp * 256:(hp + 1) * 256],
                    wtpf.rearrange("p (c o) -> p c o", c=8), 1.0,
                    whi.rearrange("p (c o) -> p c o", c=8)[
                        :, :, hp * 256:(hp + 1) * 256],
                    ALU.mult, ALU.subtract),
                  inc=VE)
                set_guard(("wtpf",), (VE, VE.n))
                lst.append((VE, VE.n))
            return lst

        def phase_A_gen(b, first_fence):
            E("vector", lambda v: v.memset(sm[:, 5:6], 0.0), inc=VE)
            qkmax = MaxAccum(5)
            ve_ln1T_tg = [0] * 4
            for t in range(NT):
                xt, lt = xt2[t % 2], lnt2[t % 2]
                LDx = dma_ctr(f"ldx{t % 2}")
                lw = guard(("xt", t % 2))
                if t == 0 and first_fence:
                    lw = lw + first_fence
                E("sync", lambda s, b=b, t=t, xt=xt: s.dma_start(
                    out=xt[:, :], in_=x_d[b, t * 128:(t + 1) * 128, :]),
                  waits=lw, inc=LDx, k=16)
                ld_x = (LDx, LDx.n)
                lnw = [ld_x] + guard(("lnt", t % 2))
                ve_ln = ln_tile(xt, lt, t, lnw)
                set_guard(("xt", t % 2), (VE, ve_ln), (SE, SE.n))
                for half in range(2):
                    bank = ps[4 + half]
                    for q in range(4):
                        cchunk = half * 4 + q
                        tw = ()
                        if q == 0:
                            tw = [(VE, ve_ln)] + guard(("psT", 4 + half))
                            if t == 0 and first_fence:
                                tw = tw + first_fence
                        E("tensor", lambda te, lt=lt, bank=bank, q=q, c=cchunk:
                          te.transpose(bank[:, q * 128:(q + 1) * 128],
                                       lt[:, c * 128:(c + 1) * 128], ident[:, :]),
                          waits=tw,
                          inc=(TE if q == 3 else None))
                    te_tr = TE.n
                    E("vector", lambda v, bank=bank, t=t, half=half:
                      v.tensor_copy(
                          ln1T_hi.rearrange("p (c w) -> p c w", c=8)[
                              :, half * 4:(half + 1) * 4, t * 128:(t + 1) * 128],
                          bank.rearrange("p (q w) -> p q w", q=4)),
                      waits=[(TE, te_tr)], inc=VE)
                    E("vector", lambda v, bank=bank, t=t, half=half:
                      v.scalar_tensor_tensor(
                          ln1T_lo.rearrange("p (c w) -> p c w", c=8)[
                              :, half * 4:(half + 1) * 4, t * 128:(t + 1) * 128],
                          bank.rearrange("p (q w) -> p q w", q=4), 1.0,
                          ln1T_hi.rearrange("p (c w) -> p c w", c=8)[
                              :, half * 4:(half + 1) * 4, t * 128:(t + 1) * 128],
                          ALU.mult, ALU.subtract),
                      inc=VE)
                    set_guard(("psT", 4 + half), (VE, VE.n))
                set_guard(("lnt", t % 2), (TE, TE.n))
                if t % 4 == 3:
                    ve_ln1T_tg[t // 4] = VE.n
                if t == 3:
                    ld_ws_p0 = emit_piece_loads(0)
                yield
            for piece in range(3):
                ld_ws = ld_ws_p0 if piece == 0 else emit_piece_loads(piece)
                for tg in range(4):
                    for oc in range(8):
                        g4 = (tg * 8 + oc) % 4
                        o_abs = piece * 1024 + oc * 128
                        bank = ps[g4]
                        stgv = stg2[g4 // 2][:, (g4 % 2) * 512:
                                             (g4 % 2) * 512 + 512]
                        mm_w = ([ld_ws[oc // 2], (VE, ve_ln1T_tg[tg])]
                                + guard(("psA", g4)))
                        for cc in range(8):
                            for p3i, (wop, xop) in enumerate(
                                    [(whi, ln1T_hi), (whi, ln1T_lo),
                                     (wlo, ln1T_hi)]):
                                E("tensor",
                                  lambda te, bank=bank, oc=oc, cc=cc, tg=tg,
                                  wop=wop, xop=xop, p3i=p3i:
                                  te.matmul(bank[:, :],
                                            wop[:, cc * 1024 + oc * 128:
                                                cc * 1024 + oc * 128 + 128],
                                            xop[:, cc * 2048 + tg * 512:
                                                cc * 2048 + tg * 512 + 512],
                                            start=(cc == 0 and p3i == 0),
                                            stop=(cc == 7 and p3i == 2)),
                                  waits=(mm_w if (cc == 0 and p3i == 0)
                                         else ()),
                                  inc=(TE if (cc == 7 and p3i == 2)
                                       else None))
                        te_mm = TE.n
                        bcol = piece * 8 + oc
                        ew = [(TE, te_mm)] + guard(("stg", g4))
                        E("scalar",
                          lambda s, bank=bank, stgv=stgv, bcol=bcol:
                          s.activation(stgv, bank[:, :], AF.Identity,
                                       bias=bias_sb[:, bcol:bcol + 1]),
                          waits=ew, inc=SE)
                        set_guard(("psA", g4), (SE, SE.n))
                        se_evac = SE.n
                        if piece < 2:
                            qkmax.reduce(stgv, waits=[(SE, se_evac)])
                        ve_all = VE.n
                        STg = dma_ctr(f"stg{g4}", store=True)
                        E("gpsimd", lambda g, b=b, o_abs=o_abs, tg=tg,
                          stgv=stgv: g.dma_start(
                              out=qkvT_d[b, o_abs:o_abs + 128,
                                         tg * 512:tg * 512 + 512],
                              in_=stgv),
                          waits=[(SE, se_evac)], inc=STg, k=16)
                        set_guard(("stg", g4), (STg, STg.n), (VE, ve_all))
                        if tg == 3 and oc % 2 == 1:
                            set_guard(("wtp", oc // 2), (TE, TE.n))
                        yield
            qkmax.finish()
            part_max(5, 4)
            vdrain()
            E("vector", lambda v: v.tensor_tensor(
                sc[0:1, 5:6], sc[0:1, 4:5], sc[0:1, 4:5], ALU.mult), inc=VE)
            vdrain()
            E("vector", lambda v: v.tensor_scalar(
                sc[0:1, 5:6], sc[0:1, 5:6], -8.0, -80.0, ALU.mult, ALU.max),
              inc=VE)
            vdrain()
            ve_b = VE.n
            bcast_col(sc[0:1, 5:6], 0, [(VE, ve_b)])

        def phase_B_gen(b, ve_bias, st_after_A):
            for bi in range(2):
                E("vector", lambda v, vtok=vtok2[bi]: v.memset(
                    vtok.rearrange("p (c d) -> p c d", d=65)[:, :, 64:65], 1.0),
                  waits=guard(("vtok", bi)), inc=VE)
                set_guard(("vtok_ones", bi), (VE, VE.n))
            for h in range(NH):
                bi = h % 2
                qb, kb, vb, vtok = qb2[bi], kb2[bi], vb2[bi], vtok2[bi]
                LDq = dma_ctr(f"ldq{bi}")
                lw = guard(("bqkv", bi))
                if h == 0:
                    lw = lw + st_after_A
                E("sync", lambda s, b=b, h=h, qb=qb: s.dma_start(
                    out=qb[:, :], in_=qkvT_d[b, h * 64:h * 64 + 64, :]),
                  waits=lw, inc=LDq, k=16)
                E("sync", lambda s, b=b, h=h, kb=kb: s.dma_start(
                    out=kb[:, :], in_=qkvT_d[b, 1024 + h * 64:1024 + h * 64 + 64, :]),
                  inc=LDq, k=16)
                E("sync", lambda s, b=b, h=h, vb=vb: s.dma_start(
                    out=vb[:, :], in_=qkvT_d[b, 2048 + h * 64:2048 + h * 64 + 64, :]),
                  inc=LDq, k=16)
                ld_qkv = (LDq, LDq.n)
                ve_ones = guard(("vtok_ones", bi))[0][1]
                for half in range(2):
                    bank = ps[4 + half]
                    for q in range(8):
                        cchunk = half * 8 + q
                        E("tensor", lambda te, vb=vb, bank=bank, q=q, c=cchunk:
                          te.transpose(bank[:, q * 64:(q + 1) * 64],
                                       vb[:, c * 128:(c + 1) * 128],
                                       ident[0:64, 0:64]),
                          waits=([ld_qkv] + guard(("psT", 4 + half))
                                 + guard(("psSD", 2))
                                 if q == 0 else ()),
                          inc=(TE if q == 7 else None))
                    te_tr = TE.n
                    E("vector", lambda v, bank=bank, vtok=vtok, half=half:
                      v.tensor_copy(
                          vtok.rearrange("p (c d) -> p c d", d=65)[
                              :, half * 8:(half + 1) * 8, 0:64],
                          bank.rearrange("p (q d) -> p q d", q=8)),
                      waits=[(TE, te_tr), (VE, ve_ones)]
                            + guard(("vtok", bi)), inc=VE)
                    set_guard(("psT", 4 + half), (VE, VE.n))
                ve_vtok = VE.n
                set_guard(("psSD", 2), (VE, ve_vtok))
                ybanks = [ps[6], ps[7]]
                if h == 0:
                    for i in (0, 1):
                        set_guard(("psSD", i),
                                  *(guard(("psA", 2 * i)) +
                                    guard(("psA", 2 * i + 1))))
                    set_guard(("psY", 1), *(guard(("psY", 1)) + guard("ps7")))
                for g in range(4):
                    nk = 4 * (g + 1)
                    npair = nk // 2
                    se_exp = [0] * npair
                    te_av = [0] * nk
                    ybank = ybanks[g % 2]

                    def emit_S(kc):
                        pr = kc // 2
                        bank = psd[pr % 3][:, (kc % 2) * 512:(kc % 2) * 512 + 512]
                        w = [ld_qkv]
                        if kc % 2 == 0:
                            w += guard(("psSD", pr % 3))
                        E("tensor", lambda te, kb=kb, qb=qb, bank=bank, kc=kc, g=g:
                          te.matmul(bank, kb[:, kc * 128:(kc + 1) * 128],
                                    qb[:, g * 512:(g + 1) * 512],
                                    start=True, stop=True), waits=w, inc=TE)
                        te_s = TE.n
                        if kc >= 4 * g:
                            r = kc * 128 - g * 512
                            E("vector", lambda v, bank=bank, r=r: v.tensor_tensor(
                                bank, bank,
                                maskbig[:, 384 - r:384 - r + 512], ALU.add),
                              waits=[(TE, te_s)], inc=VE)
                            return (VE, VE.n)
                        return (TE, te_s)

                    def emit_exp(pr, ready):
                        w = list(ready) + [(VE, ve_bias)]
                        if pr >= 2:
                            w.append((TE, te_av[2 * (pr - 2) + 1]))
                        E("scalar", lambda s, pr=pr: s.activation(
                            pbD[pr % 2][:, :], psd[pr % 3][:, :], AF.Exp,
                            scale=0.125, bias=sm[:, 0:1]), waits=w, inc=SE)
                        se_exp[pr] = SE.n
                        set_guard(("psSD", pr % 3), (SE, SE.n))

                    def emit_AV(kc):
                        pr = kc // 2
                        w = [(SE, se_exp[pr]), (VE, ve_vtok)]
                        if kc == 0:
                            w += guard(("psY", g % 2))
                        E("tensor", lambda te, vtok=vtok, kc=kc, nk=nk, yb=ybank,
                          pr=pr:
                          te.matmul(yb[0:65, :], vtok[:, kc * 65:kc * 65 + 65],
                                    pbD[pr % 2][:, (kc % 2) * 512:
                                                 (kc % 2) * 512 + 512],
                                    start=(kc == 0), stop=(kc == nk - 1)),
                          waits=w, inc=TE)
                        te_av[kc] = TE.n

                    readies = {}
                    for kc in (0, 1, 2, 3, 4, 5):
                        if kc < nk:
                            readies[kc] = emit_S(kc)
                    for pr in range(npair):
                        emit_exp(pr, [readies.pop(2 * pr),
                                      readies.pop(2 * pr + 1)])
                        for kc in (2 * pr + 6, 2 * pr + 7):
                            if kc < nk:
                                readies[kc] = emit_S(kc)
                        emit_AV(2 * pr)
                        emit_AV(2 * pr + 1)
                    te_y = TE.n
                    yst = yst2[g % 2]
                    E("vector", lambda v, yst=yst, yb=ybank: v.tensor_copy(
                        yst[0:65, :], yb[0:65, :]),
                      waits=[(TE, te_y)] + guard(("yst", g % 2)), inc=VE)
                    set_guard(("psY", g % 2), (VE, VE.n))
                    ve_y = VE.n
                    STy = dma_ctr(f"sty{g % 2}", store=True)
                    E("gpsimd", lambda gp, b=b, h=h, g=g, yst=yst: gp.dma_start(
                        out=y_d[b, h, :, g * 512:(g + 1) * 512], in_=yst[0:65, :]),
                      waits=[(VE, ve_y)], inc=STy, k=16)
                    set_guard(("yst", g % 2), (STy, STy.n))
                set_guard(("bqkv", bi), (TE, TE.n))
                set_guard(("vtok", bi), (TE, TE.n))
                yield
            for i in range(4):
                set_guard(("psA", i), (SE, SE.n), (VE, VE.n))
            set_guard(("psT", 4), (SE, SE.n))
            set_guard(("psT", 5), (SE, SE.n))
            set_guard("ps7", *guard(("psY", 1)))

        def b_stores():
            return [(c, c.n) for c in (dma_ctr("sty0"), dma_ctr("sty1"))]

        def phase_F1_gen(b, st_after_B):
            fbanks = [ps[6], ps[7]]
            for t in range(NT):
                xt, yl = fxt2[t % 2], fyTl2[t % 2]
                o1, lt = fo1b2[t % 2], flnt2[t % 2]
                LDx = dma_ctr(f"fldx{t % 2}")
                LDy = dma_ctr(f"fldy{t % 2}")
                E("sync", lambda s, b=b, t=t, xt=xt: s.dma_start(
                    out=xt[:, :], in_=x_d[b, t * 128:(t + 1) * 128, :]),
                  waits=guard(("fxt", t % 2)), inc=LDx, k=16)
                ld_x = (LDx, LDx.n)
                ylw = guard(("fyTl", t % 2))
                if t == 0:
                    ylw = ylw + st_after_B
                E("sync", lambda s, b=b, t=t, yl=yl: s.dma_start(
                    out=yl.rearrange("p (h c) -> p h c", h=16),
                    in_=y_d.rearrange("b h r c -> b r h c")[
                        b, :, :, t * 128:(t + 1) * 128]),
                  waits=ylw, inc=LDy, k=16)
                ld_y = (LDy, LDy.n)
                o1w = guard(("fo1b", t % 2))

                def emit_grp_stt(grp):
                    bank = fbanks[grp % 2]
                    rb = 16 + 4 * (grp % 2)
                    for j in range(4):
                        h = grp * 4 + j
                        E("vector", lambda v, o1=o1, xt=xt, h=h, j=j, bank=bank,
                          rb=rb:
                          v.scalar_tensor_tensor(
                              o1[:, h * 64:(h + 1) * 64],
                              bank[:, j * 65:j * 65 + 64],
                              sm[:, rb + j:rb + j + 1],
                              xt[:, h * 64:(h + 1) * 64], ALU.mult, ALU.add),
                          waits=([ld_x] + (o1w if h == 0 else [])
                                 if j == 0 else ()), inc=VE)
                    set_guard(("fps", grp % 2), (VE, VE.n))

                for grp in range(4):
                    bank = fbanks[grp % 2]
                    rb = 16 + 4 * (grp % 2)
                    for j in range(4):
                        h = grp * 4 + j
                        gw = []
                        if j == 0:
                            gw = [ld_y] + guard(("fps", grp % 2))
                            if grp % 2 == 0:
                                gw += guard(("psY", 0))
                            else:
                                gw += guard(("psY", 1)) + guard("ps7")
                        E("tensor", lambda te, yl=yl, h=h, j=j, bank=bank:
                          te.transpose(bank[:, j * 65:j * 65 + 65],
                                       yl[:, h * 128:(h + 1) * 128],
                                       ident[0:65, 0:65]),
                          waits=gw, inc=(TE if j == 3 else None))
                    te_tr = TE.n
                    if grp % 2 == 0:
                        set_guard(("psY", 0), (TE, te_tr))
                    else:
                        set_guard(("psY", 1), (TE, te_tr))
                        set_guard("ps7", (TE, te_tr))
                    E("vector", lambda v, bank=bank, rb=rb: v.reciprocal(
                        sm[:, rb:rb + 4],
                        bank[:, 0:260].rearrange("p (j c) -> p j c", c=65)[
                            :, :, 64:65]),
                      waits=[(TE, te_tr)], inc=VE)
                    vdrain()
                    if grp >= 1:
                        emit_grp_stt(grp - 1)
                emit_grp_stt(3)
                ve_o1 = VE.n
                set_guard(("fxt", t % 2), (VE, ve_o1))
                set_guard(("fyTl", t % 2), (TE, TE.n))
                set_guard(("fps", 0), *(guard(("fps", 0)) + [(VE, ve_o1)]))
                set_guard(("fps", 1), *(guard(("fps", 1)) + [(VE, ve_o1)]))
                set_guard(("psY", 0), *(guard(("psY", 0)) + [(VE, ve_o1)]))
                set_guard(("psY", 1), *(guard(("psY", 1)) + [(VE, ve_o1)]))
                set_guard("ps7", *(guard("ps7") + [(VE, ve_o1)]))
                STo = dma_ctr(f"fsto{t % 2}", store=True)
                STl = dma_ctr(f"fstl{t % 2}", store=True)
                E("gpsimd", lambda g, b=b, t=t, o1=o1: g.dma_start(
                    out=out1_d[b, t * 128:(t + 1) * 128, :], in_=o1[:, :]),
                  waits=[(VE, ve_o1)], inc=STo, k=16)
                st_o1 = (STo, STo.n)
                ve_ln = ln_tile(o1, lt, t, [(VE, ve_o1)] + guard(("flnt", t % 2)),
                                so=64, tb0=24, sqb=fsq)
                set_guard(("fo1b", t % 2), st_o1, (SE, SE.n), (VE, ve_ln))
                xmax.reduce(lt[:, :])
                E("gpsimd", lambda g, b=b, t=t, lt=lt: g.dma_start(
                    out=ln2_d[b, t * 128:(t + 1) * 128, :], in_=lt[:, :]),
                  waits=[(VE, VE.n)], inc=STl, k=16)
                set_guard(("flnt", t % 2), (STl, STl.n))
                yield
            xmax.finish()
            fin = ([(VE, VE.n), (SE, SE.n), (TE, TE.n)]
                   + [(c, c.n) for c in (dma_ctr("fsto0"), dma_ctr("fsto1"),
                                         dma_ctr("fstl0"), dma_ctr("fstl1"))])
            for key in [("bqkv", 0), ("bqkv", 1), ("vtok", 0), ("vtok", 1),
                        ("psY", 0), ("psY", 1), "ps7"]:
                set_guard(key, *(guard(key) + fin))

        def f1_stores():
            return [(c, c.n) for c in (dma_ctr("fsto0"), dma_ctr("fsto1"),
                                       dma_ctr("fstl0"), dma_ctr("fstl1"))]

        xmax = MaxAccum(12, 30)
        hmax = MaxAccum(13, 14)

        def collective_max(col, sig_col, rq_col, s_col):
            STcc = dma_ctr("stcc", store=True)
            part_max(col, 6)
            ve_m = VE.n
            E("gpsimd", lambda g: g.dma_start(out=cci_d[:, :], in_=sc[0:1, 6:7]),
              waits=[(VE, ve_m)], inc=STcc, k=16)
            st_in = STcc.n
            E("gpsimd", lambda g: g.collective_compute(
                "AllReduce", ALU.max, replica_groups=[list(range(NCORES))],
                ins=[cci_d[:, :].opt()], outs=[cco_d[:, :].opt()]),
              waits=[(STcc, st_in)], inc=CC)
            cc_done = CC.n
            E("gpsimd", lambda g: g.dma_start(out=sc[0:1, 7:8], in_=cco_d[:, :]),
              waits=[(CC, cc_done)], inc=STcc, k=16)
            st_out = (STcc, STcc.n)
            E("vector", lambda v, sig_col=sig_col: v.tensor_tensor(
                sc[0:1, 8:9], sc[0:1, 7:8], sc[0:1, sig_col:sig_col + 1],
                ALU.mult), waits=[st_out, (SE, se_sig)], inc=VE)
            vdrain()
            E("vector", lambda v: v.reciprocal(sc[0:1, 9:10], sc[0:1, 8:9]), inc=VE)
            vdrain()
            E("vector", lambda v: v.tensor_scalar(
                sc[0:1, 9:10], sc[0:1, 9:10], QMAX, None, ALU.mult), inc=VE)
            E("vector", lambda v: v.tensor_scalar(
                sc[0:1, 8:9], sc[0:1, 8:9], 1.0 / QMAX, None, ALU.mult), inc=VE)
            vdrain()
            ve_rq = VE.n
            bcast_col(sc[0:1, 9:10], rq_col, [(VE, ve_rq)])
            return bcast_col(sc[0:1, 8:9], s_col, [])

        def wquant_gen(wn_dram, nch_n, nfree_n, scol, codes_dst, wnt, ci8_sb,
                       cf_sb, first_fence, tbank0=4):
            nch_t = nfree_n // 128
            ncol = nch_n * 128
            for cc in range(nch_n):
                buf = wnt[cc % 2]
                LDwq = dma_ctr(f"ldwq_{buf.tensor.name}")
                lw = guard(("wqb", buf.tensor.name))
                if cc == 0 and first_fence:
                    lw = lw + first_fence
                E("sync", lambda s, cc=cc, buf=buf, nf=nfree_n, wn=wn_dram:
                  s.dma_start(out=buf[:, 0:nf],
                              in_=wn[cc * 128:(cc + 1) * 128, :]),
                  waits=lw, inc=LDwq, k=16)
                ld_w = (LDwq, LDwq.n)
                E("vector", lambda v, cc=cc, buf=buf, nf=nfree_n:
                  v.tensor_reduce(scl[:, scol + cc:scol + cc + 1],
                                  buf[:, 0:nf], AX.X, ALU.max,
                                  apply_absolute_value=True),
                  waits=[ld_w], inc=VE)
                vdrain()
                E("vector", lambda v, cc=cc: v.tensor_scalar(
                    scl[:, scol + cc:scol + cc + 1],
                    scl[:, scol + cc:scol + cc + 1], 1.0 / QMAX, None,
                    ALU.mult), inc=VE)
                vdrain()
                E("vector", lambda v, cc=cc: v.reciprocal(
                    scl[:, 104:105], scl[:, scol + cc:scol + cc + 1]), inc=VE)
                vdrain()
                E("vector", lambda v, cc=cc, buf=buf, nf=nfree_n, c8=ci8_sb:
                  v.tensor_scalar(c8[:, 0:nf], buf[:, 0:nf], scl[:, 104:105],
                                  None, ALU.mult),
                  waits=guard(("wq8", ci8_sb.tensor.name)), inc=VE)
                set_guard(("wqb", buf.tensor.name), (VE, VE.n))
                E("vector", lambda v, c8=ci8_sb, cf=cf_sb, nf=nfree_n:
                  v.tensor_copy(cf[:, 0:nf], c8[:, 0:nf]),
                  waits=guard(("wqf", cf_sb.tensor.name)), inc=VE)
                ve_cf = VE.n
                set_guard(("wq8", ci8_sb.tensor.name), (VE, VE.n))
                for j4 in range(0, nch_t, 4):
                    nj = min(4, nch_t - j4)
                    bank = ps[tbank0 + (j4 // 4) % 2]
                    for q in range(nj):
                        j = j4 + q
                        E("tensor", lambda te, cf=cf_sb, bank=bank, q=q, j=j:
                          te.transpose(bank[:, q * 128:(q + 1) * 128],
                                       cf[:, j * 128:(j + 1) * 128],
                                       ident[:, :]),
                          waits=([(VE, ve_cf)] +
                                 guard(("psT", tbank0 + (j4 // 4) % 2))
                                 if q == 0 else ()),
                          inc=(TE if q == nj - 1 else None))
                    te_tr = TE.n
                    E("vector", lambda v, bank=bank, cc=cc, j4=j4, nj=nj,
                      cd=codes_dst, ncol=ncol:
                      v.tensor_copy(
                          cd.rearrange("p (j o) -> p j o", o=ncol)[
                              :, j4:j4 + nj, cc * 128:(cc + 1) * 128],
                          bank[:, 0:nj * 128].rearrange(
                              "p (q w) -> p q w", q=nj)),
                      waits=[(TE, te_tr)], inc=VE)
                    set_guard(("psT", tbank0 + (j4 // 4) % 2), (VE, VE.n))
                set_guard(("wqf", cf_sb.tensor.name), (TE, TE.n))
                yield


        def drive(gens, pattern):
            alive = [iter(g) for g in gens]
            while alive:
                for gi, g in enumerate(list(alive)):
                    n = pattern[min(gi, len(pattern) - 1)]
                    for _ in range(n):
                        try:
                            next(g)
                        except StopIteration:
                            alive.remove(g)
                            break

        def a_values(b):
            return VE.n, [(c, c.n) for c in
                          (dma_ctr("stg0"), dma_ctr("stg1"),
                           dma_ctr("stg2"), dma_ctr("stg3"))]

        drive([phase_A_gen(0, None)], [99])
        ve_bias0, st_a0 = a_values(0)
        drive([phase_B_gen(0, ve_bias0, st_a0)], [99])
        st_b0 = b_stores()
        if INTERLEAVE_AF:
            drive([phase_A_gen(1, fence()), phase_F1_gen(0, st_b0)], [7, 1])
        else:
            drive([phase_F1_gen(0, st_b0)], [99])
            drive([phase_A_gen(1, fence())], [99])
        ve_bias1, st_a1 = a_values(1)
        drive([phase_B_gen(1, ve_bias1, st_a1)], [99])
        st_b1 = b_stores()
        wq_fence = fence()
        if INTERLEAVE_WQ:
            drive([phase_F1_gen(1, st_b1),
                   wquant_gen(wg_n_d, NIC, C, 0, wcg, wnt2, ci8, wq_cf,
                              wq_fence),
                   wquant_gen(wu_n_d, NIC, C, 22, wcu, wnt2, ci8, wq_cf,
                              None)],
                  [1, 2, 2])
        else:
            drive([phase_F1_gen(1, st_b1)], [99])
            drive([wquant_gen(wg_n_d, NIC, C, 0, wcg, wnt2, ci8, wq_cf,
                              fence())], [99])
            drive([wquant_gen(wu_n_d, NIC, C, 22, wcu, wnt2, ci8, wq_cf,
                              None)], [99])
        st_f1 = f1_stores()
        cc1 = collective_max(12, 2, 1, 3)
        vdrain()
        E("vector", lambda v: v.tensor_scalar(
            scl[:, 52:74], scl[:, 0:22], sm[:, 3:4], None, ALU.mult),
          waits=[(VE, cc1)], inc=VE)
        E("vector", lambda v: v.tensor_scalar(
            scl[:, 74:96], scl[:, 22:44], sm[:, 3:4], None, ALU.mult), inc=VE)
        ve_sxw = VE.n

        p2_fence = fence()
        for tt in range(2 * NT):
            b, t = divmod(tt, NT)
            lt = ln2l2[tt % 2]
            LDl = dma_ctr(f"ldl{tt % 2}")
            lw = guard(("p2ln", tt % 2))
            if tt == 0:
                lw = lw + st_f1 + p2_fence
            E("sync", lambda s, b=b, t=t, lt=lt: s.dma_start(
                out=lt[:, :], in_=ln2_d[b, t * 128:(t + 1) * 128, :]),
              waits=lw, inc=LDl, k=16)
            ld_ln = (LDl, LDl.n)
            E("vector", lambda v, lt=lt: v.tensor_scalar(
                xc8[:, :], lt[:, :], sm[:, 1:2], None, ALU.mult),
              waits=[ld_ln, (VE, cc1)] + guard("xc8"), inc=VE)
            E("vector", lambda v: v.tensor_copy(xcf[:, :], xc8[:, :]),
              waits=guard("xcf"), inc=VE)
            ve_xcb = VE.n
            set_guard(("p2ln", tt % 2), (VE, ve_xcb))
            set_guard("xc8", (VE, ve_xcb))
            for half in range(2):
                bank = ps[4 + half]
                for q in range(4):
                    cchunk = half * 4 + q
                    tw = []
                    if q == 0:
                        tw = [(VE, ve_xcb)] + guard(("psT", 4 + half))
                    E("tensor", lambda te, q=q, c=cchunk, bank=bank:
                      te.transpose(bank[:, q * 128:(q + 1) * 128],
                                   xcf[:, c * 128:(c + 1) * 128], ident[:, :]),
                      waits=tw, inc=(TE if q == 3 else None))
                te_tr = TE.n
                E("vector", lambda v, tt=tt, half=half, bank=bank: v.tensor_copy(
                    xcT.rearrange("p (c w) -> p c w", c=8)[
                        :, half * 4:(half + 1) * 4, tt * 128:(tt + 1) * 128],
                    bank.rearrange("p (q w) -> p q w", q=4)),
                  waits=[(TE, te_tr)], inc=VE)
                set_guard(("psT", 4 + half), (VE, VE.n))
            set_guard("xcf", (TE, TE.n))
        ve_xcT = VE.n

        for tg in range(TG_ALL):
            for oc in range(NIC):
                psG = ps[oc % 2]
                psU = ps[2 + oc % 2]
                gkey = ("psA", oc % 2)
                ukey = ("psA", 2 + oc % 2)
                hb = hb3[(tg * NIC + oc) % 3]
                sg = sgb2[oc % 2]
                mm_w = [(VE, ve_xcT), (VE, ve_sxw)] + guard(gkey)
                for cc in range(NCC):
                    E("tensor", lambda te, psG=psG, oc=oc, cc=cc, tg=tg:
                      te.matmul(psG[:, :],
                                wcg[:, cc * IFF + oc * 128:
                                    cc * IFF + oc * 128 + 128],
                                xcT[:, cc * 4096 + tg * 512:
                                    cc * 4096 + tg * 512 + 512],
                                start=(cc == 0), stop=(cc == NCC - 1)),
                      waits=mm_w if cc == 0 else (),
                      inc=(TE if cc == NCC - 1 else None))
                te_g = TE.n
                mm_w2 = guard(ukey)
                for cc in range(NCC):
                    E("tensor", lambda te, psU=psU, oc=oc, cc=cc, tg=tg:
                      te.matmul(psU[:, :],
                                wcu[:, cc * IFF + oc * 128:
                                    cc * IFF + oc * 128 + 128],
                                xcT[:, cc * 4096 + tg * 512:
                                    cc * 4096 + tg * 512 + 512],
                                start=(cc == 0), stop=(cc == NCC - 1)),
                      waits=mm_w2 if cc == 0 else (),
                      inc=(TE if cc == NCC - 1 else None))
                te_u = TE.n
                E("scalar", lambda s, psG=psG, sg=sg, oc=oc: s.activation(
                    sg[:, :], psG[:, :], AF.Silu,
                    scale=scl[:, 52 + oc:53 + oc]),
                  waits=[(TE, te_g)] + guard(("sgb", oc % 2)), inc=SE)
                set_guard(gkey, (SE, SE.n))
                se_sg = SE.n
                E("vector", lambda v, psU=psU, sg=sg, hb=hb, oc=oc:
                  v.scalar_tensor_tensor(hb[:, :], psU[:, :],
                                         scl[:, 74 + oc:75 + oc], sg[:, :],
                                         ALU.mult, ALU.mult),
                  waits=[(TE, te_u), (SE, se_sg)] +
                        guard(("hb", (tg * NIC + oc) % 3)), inc=VE)
                set_guard(ukey, (VE, VE.n))
                set_guard(("sgb", oc % 2), (VE, VE.n))
                hmax.reduce(hb[:, :])
                ve_h = VE.n
                STh = dma_ctr(f"sth{(tg * NIC + oc) % 3}", store=True)
                E("gpsimd", lambda g, oc=oc, tg=tg, hb=hb: g.dma_start(
                    out=h_d[oc * 128:(oc + 1) * 128, tg * 512:(tg + 1) * 512],
                    in_=hb[:, :]), waits=[(VE, ve_h)], inc=STh, k=16)
                set_guard(("hb", (tg * NIC + oc) % 3), (STh, STh.n))
        st_p2 = [(c, c.n) for c in (dma_ctr("sth0"), dma_ctr("sth1"),
                                    dma_ctr("sth2"))]
        te_p2 = TE.n

        if debug:
            for name, src in [("d_xcT", xcT), ("d_wcg", wcg), ("d_wcu", wcu),
                              ("d_scl", scl), ("d_sm", sm), ("d_sc", sc)]:
                shp = list(src.shape)
                dt = BF16 if name in ("d_xcT", "d_wcg", "d_wcu") else F32
                dd = nc.dram_tensor(name, shp, dt, kind="ExternalOutput").ap()
                STd = dma_ctr("stdbg", store=True)
                E("gpsimd", lambda g, dd=dd, src=src: g.dma_start(
                    out=dd, in_=src), waits=fence(), inc=STd, k=16)

        drive([wquant_gen(wd_n_d, NCC, IFF, 44, wcd, wnt3, ci83, wq_cf3,
                          [(TE, te_p2)] + all_stores())], [99])
        wq_d = VE.n
        hmax.finish()
        cc2 = collective_max(13, 3, 2, 4)
        vdrain()
        E("vector", lambda v: v.tensor_scalar(
            scl[:, 96:104], scl[:, 44:52], sm[:, 4:5], None, ALU.mult),
          waits=[(VE, cc2)], inc=VE)
        ve_sdh = VE.n

        def p3_quant(tg):
            b = tg // 4
            nhb = nhb2[tg % 2]
            for ic in range(NIC):
                hl = hl2[ic % 2]
                LDh = dma_ctr(f"ldh{ic % 2}")
                lw = guard(("hl", ic % 2))
                if tg == 0 and ic == 0:
                    lw = lw + st_p2
                E("sync", lambda s, ic=ic, tg=tg, hl=hl: s.dma_start(
                    out=hl[:, :],
                    in_=h_d[ic * 128:(ic + 1) * 128, tg * 512:(tg + 1) * 512]),
                  waits=lw, inc=LDh, k=16)
                ld_h = (LDh, LDh.n)
                qw = [ld_h, (VE, cc2)] + guard("ci83")
                if ic == 0:
                    qw += guard(("nhb", tg % 2))
                E("vector", lambda v, hl=hl: v.tensor_scalar(
                    ci83[:, 0:512], hl[:, :], sm[:, 2:3], None, ALU.mult),
                  waits=qw, inc=VE)
                E("vector", lambda v, ic=ic, nhb=nhb: v.tensor_copy(
                    nhb[:, ic * 512:(ic + 1) * 512], ci83[:, 0:512]), inc=VE)
                set_guard(("hl", ic % 2), (VE, VE.n))
                set_guard("ci83", (VE, VE.n))
            return nhb, VE.n

        def p3_mmpost(tg, nhb, ve_nh):
            b = tg // 4
            for q in range(4):
                t_loc = (tg % 4) * 4 + q
                LDo1 = dma_ctr(f"ldo1c{q}")
                E("gpsimd", lambda s, b=b, t_loc=t_loc, o1c=o1c4[q]:
                  s.dma_start(
                      out=o1c[:, :],
                      in_=out1_d[b, t_loc * 128:(t_loc + 1) * 128, :]),
                  waits=guard(("o1c", q)), inc=LDo1, k=16)
                set_guard(("o1cld", q), (LDo1, LDo1.n))

            se_dsc = [0] * NCC

            def emit_mms(fc):
                psD = ps[fc % 2]
                mm_w = [(VE, ve_nh), (VE, wq_d)] + guard(("psD", fc % 2))
                for ic in range(NIC):
                    E("tensor", lambda te, psD=psD, fc=fc, ic=ic, nhb=nhb:
                      te.matmul(psD[:, :],
                                wcd[:, ic * 1024 + fc * 128:
                                    ic * 1024 + fc * 128 + 128],
                                nhb[:, ic * 512:(ic + 1) * 512],
                                start=(ic == 0), stop=(ic == NIC - 1)),
                      waits=mm_w if ic == 0 else (),
                      inc=(TE if ic == NIC - 1 else None))
                te_d = TE.n
                E("scalar", lambda s, psD=psD, dsc=dsc2[fc % 2], fc=fc:
                  s.activation(dsc[:, :], psD[:, :], AF.Identity,
                               scale=scl[:, 96 + fc:97 + fc]),
                  waits=[(TE, te_d), (VE, ve_sdh)] + guard(("dsc", fc % 2)),
                  inc=SE)
                set_guard(("psD", fc % 2), (SE, SE.n))
                se_dsc[fc] = SE.n

            def emit_post(fc):
                dsc = dsc2[fc % 2]
                bank = ps[4 + fc % 2]
                for q in range(4):
                    E("tensor", lambda te, dsc=dsc, bank=bank, q=q:
                      te.transpose(bank[:, q * 128:(q + 1) * 128],
                                   dsc[:, q * 128:(q + 1) * 128], ident[:, :]),
                      waits=([(SE, se_dsc[fc])] + guard(("psT", 4 + fc % 2))
                             if q == 0 else ()),
                      inc=(TE if q == 3 else None))
                set_guard(("dsc", fc % 2), (TE, TE.n))
                te_tr = TE.n
                for q in range(4):
                    t_loc = (tg % 4) * 4 + q
                    o1c = o1c4[q]
                    o2 = o2b4[q]
                    sw = [(TE, te_tr)] + guard(("o1cld", q))
                    if fc == 0:
                        sw += guard(("o2b", q))
                    E("vector", lambda v, o2=o2, o1c=o1c, bank=bank, q=q, fc=fc:
                      v.scalar_tensor_tensor(
                          o2[:, fc * 128:(fc + 1) * 128],
                          bank[:, q * 128:(q + 1) * 128], 1.0,
                          o1c[:, fc * 128:(fc + 1) * 128], ALU.mult, ALU.add),
                      waits=sw, inc=VE)
                    if fc == NCC - 1:
                        set_guard(("o1c", q), (VE, VE.n))
                        ve_o2 = VE.n
                        STo2 = dma_ctr(f"sto2_{q}", store=True)
                        E("gpsimd", lambda g, b=b, t_loc=t_loc, o2=o2:
                          g.dma_start(
                              out=out_d[b, t_loc * 128:(t_loc + 1) * 128, :],
                              in_=o2[:, :]), waits=[(VE, ve_o2)], inc=STo2, k=16)
                        set_guard(("o2b", q), (STo2, STo2.n))
                set_guard(("psT", 4 + fc % 2), (VE, VE.n))

            for fc in range(NCC):
                emit_mms(fc)
                if fc >= 1:
                    emit_post(fc - 1)
            emit_post(NCC - 1)
            set_guard(("nhb", tg % 2), (TE, TE.n))
            if debug and tg == 2:
                dnh = nc.dram_tensor("d_nh2", [128, NIC * 512], BF16,
                                     kind="ExternalOutput").ap()
                STd = dma_ctr("stdbg", store=True)
                E("gpsimd", lambda g, dnh=dnh, nhb=nhb: g.dma_start(
                    out=dnh, in_=nhb), waits=[(VE, ve_nh)], inc=STd, k=16)
                set_guard(("nhb", tg % 2), (TE, TE.n), (STd, STd.n))

        pending = None
        for tg in range(TG_ALL):
            state = p3_quant(tg)
            if pending is not None:
                p3_mmpost(*pending)
            pending = (tg, state[0], state[1])
        p3_mmpost(*pending)

        if debug:
            STd = dma_ctr("stdbg", store=True)
            dsm = nc.dram_tensor("d_sm_end", [128, 32], F32,
                                 kind="ExternalOutput").ap()
            E("gpsimd", lambda g: g.dma_start(out=dsm, in_=sm),
              waits=fence(), inc=STd, k=16)
            for name, src in [("d_qkvT", qkvT_d), ("d_y", y_d),
                              ("d_out1", out1_d), ("d_ln2", ln2_d),
                              ("d_h", h_d)]:
                dst = nc.dram_tensor(name, list(src.shape), F32,
                                     kind="ExternalOutput").ap()
                E("gpsimd", lambda g, dst=dst, src=src: g.dma_start(
                    out=dst, in_=src), waits=fence(), inc=STd, k=16)

        for c, v in all_stores():
            if v > 0:
                E("gpsimd", lambda g, c=c, v=v: g.wait_ge(c.sem, v))

        per_engine = {"sync": [], "gpsimd": [], "tensor": [], "vector": [],
                      "scalar": []}
        for op in ops:
            per_engine[op.engine].append(op)
        produced_by = {TE.sem: "tensor", VE.sem: "vector", SE.sem: "scalar"}

        def make_body(eng_name, eng_ops):
            def body(eng):
                seen = {}
                for op in eng_ops:
                    for (sem, val, _nm) in op.waits:
                        if produced_by.get(sem) == eng_name:
                            continue
                        if seen.get(sem, -1) >= val:
                            continue
                        eng.wait_ge(sem, val)
                        seen[sem] = val
                    bi = op.emit(eng)
                    for (sem, k) in op.incs:
                        bi.then_inc(sem, k)
            return body

        with nc.Block() as block:
            block.sync(make_body("sync", per_engine["sync"]))
            block.gpsimd(make_body("gpsimd", per_engine["gpsimd"]))
            block.tensor(make_body("tensor", per_engine["tensor"]))
            block.vector(make_body("vector", per_engine["vector"]))
            block.scalar(make_body("scalar", per_engine["scalar"]))

    return nc


def _host_inputs(L0_attn_x, L0_attn_W, L0_attn_b, L0_ffn_clip_x,
                 L0_ffn_clip_down, L0_ffn_w_gate, L0_ffn_w_up, L0_ffn_w_down):
    import ml_dtypes
    x = np.ascontiguousarray(np.asarray(L0_attn_x, dtype=np.float32))
    W = np.asarray(L0_attn_W, dtype=np.float32)
    b = np.asarray(L0_attn_b, dtype=np.float32)
    wg = np.asarray(L0_ffn_w_gate, dtype=np.float32)
    wu = np.asarray(L0_ffn_w_up, dtype=np.float32)
    wd = np.asarray(L0_ffn_w_down, dtype=np.float32)
    u = np.arange(896)[None, :]
    p = np.arange(128)[:, None]
    common = {
        "wt": np.ascontiguousarray(W.T),
        "bias_sb": np.ascontiguousarray(b.reshape(24, 128).T),
        "wg_n": np.ascontiguousarray(wg),
        "wu_n": np.ascontiguousarray(wu),
        "wd_n": np.ascontiguousarray(wd),
        "wg_t": np.ascontiguousarray(wg.T),
        "wu_t": np.ascontiguousarray(wu.T),
        "wd_t": np.ascontiguousarray(wd.T),
        "clip2": np.array([[np.asarray(L0_ffn_clip_x).reshape(-1)[0],
                            np.asarray(L0_ffn_clip_down).reshape(-1)[0]]],
                          dtype=np.float32),
        "ident": np.eye(128, dtype=np.float32),
        "identb": np.eye(128, dtype=np.float32).astype(ml_dtypes.bfloat16),
        "ones1": np.ones((1, 128), dtype=np.float32),
        "maskbig": np.where(u - p >= 384, np.float32(0.0),
                            np.float32(MASK_NEG)).astype(np.float32),
    }
    return [dict(common, x=x[2 * i:2 * i + 2]) for i in range(NCORES)]


_NC_CACHE = {}


def kernel(**inputs):
    in_maps = _host_inputs(**inputs)
    if "nc" not in _NC_CACHE:
        _NC_CACHE["nc"] = build_program()
    nc = _NC_CACHE["nc"]
    res = run_bass_kernel_spmd(nc, in_maps, core_ids=list(range(NCORES)))
    out = np.concatenate([np.asarray(r["out"]) for r in res.results], axis=0)
    return np.ascontiguousarray(out.astype(np.float32))
